# version 12
# baseline (speedup 1.0000x reference)
import math
import numpy as np
import ml_dtypes
from contextlib import ExitStack

import concourse.bass as bass
import concourse.mybir as mybir
from concourse.bass_utils import run_bass_kernel_spmd

F32 = mybir.dt.float32
BF16 = mybir.dt.bfloat16
AF = mybir.ActivationFunctionType
ALU = mybir.AluOpType
AX = mybir.AxisListType.X

FULL = dict(B=2, S=2048, D=2048, H=16, DH=128, WIN=1024, CAP=30.0,
            ISH=8192, IR=1024, NEXP=8, EPS=1e-5)
SMALL = dict(B=1, S=512, D=256, H=2, DH=128, WIN=256, CAP=30.0,
             ISH=512, IR=256, NEXP=8, EPS=1e-5)
MED = dict(B=1, S=2048, D=256, H=2, DH=128, WIN=1024, CAP=30.0,
           ISH=512, IR=256, NEXP=8, EPS=1e-5)

NEG = -60.0


class Ctr:
    def __init__(self, sem):
        self.sem = sem
        self.n = 0

    def inc(self, inst, k=1):
        inst.then_inc(self.sem, k)
        self.n += k
        return self.n

    def dma(self, inst):
        return self.inc(inst, 16)


def host_consts(cfg):
    j = np.arange(128)[:, None]
    i = np.arange(128)[None, :]
    mask_c = np.where(j <= i, 0.0, NEG).astype(np.float32)
    mask_e = np.where(j > i, 0.0, NEG).astype(np.float32)
    ones_bf = np.ones((128, 1), dtype=ml_dtypes.bfloat16)
    ones_f32r = np.ones((1, 128), dtype=np.float32)
    mask_cb = (j <= i).astype(ml_dtypes.bfloat16)
    mask_eb = (j > i).astype(ml_dtypes.bfloat16)
    return {"mask_c": mask_c, "mask_e": mask_e, "ones_bf": ones_bf,
            "ones_f32r": ones_f32r, "mask_cb": mask_cb, "mask_eb": mask_eb}


def build(cfg):
    B, S, D, H, DH = cfg["B"], cfg["S"], cfg["D"], cfg["H"], cfg["DH"]
    WIN, CAP, ISH, IR = cfg["WIN"], cfg["CAP"], cfg["ISH"], cfg["IR"]
    NEXP, EPS = cfg["NEXP"], cfg["EPS"]
    SCALE = 1.0 / math.sqrt(DH)
    HT, ST, WT = D // 128, S // 128, WIN // 128
    BS, BST = B * S, B * ST
    IT, IRT = ISH // 128, IR // 128
    SUP = min(256, ISH)
    TC = min(1024, BS)
    NTC = BS // TC
    OH = D // 2
    CW = min(512, S)
    CPT = CW // 128
    NCH = S // CW
    DCK = (D + 511) // 512
    OCK = (OH + 511) // 512
    TCK = TC // 512 if TC >= 512 else 1
    assert D == H * DH and DH == 128

    def NIT(jt):
        return min(jt + WT, ST - 1) - jt + 1

    SLOTW = (WT + 1) * 128

    mk_at = {}
    run = 0
    for b in range(B):
        for hd in range(H):
            for jt in range(ST):
                run += 1 + (1 if NIT(jt) == WT + 1 else 0)
                mk_at[(b, hd, jt)] = run
    gps = (SUP // 128) * NTC
    nsup_sh, nsup_r = ISH // SUP, IR // SUP
    gu_done = [2 * gps * (i + 1) for i in range(nsup_sh + nsup_r)]

    nc = bass.Bass("TRN2", target_bir_lowering=False)

    x_d = nc.dram_tensor("x", [BS, D], F32, kind="ExternalInput").ap()
    wq_d = nc.dram_tensor("wq", [D, D], F32, kind="ExternalInput").ap()
    wk_d = nc.dram_tensor("wk", [D, D], F32, kind="ExternalInput").ap()
    wv_d = nc.dram_tensor("wv", [D, D], F32, kind="ExternalInput").ap()
    wo_d = nc.dram_tensor("wo", [D, D], F32, kind="ExternalInput").ap()
    r_d = nc.dram_tensor("router", [NEXP, D], F32, kind="ExternalInput").ap()
    gsh_d = nc.dram_tensor("gsh", [ISH, D], F32, kind="ExternalInput").ap()
    ush_d = nc.dram_tensor("ush", [ISH, D], F32, kind="ExternalInput").ap()
    dsh_d = nc.dram_tensor("dsh", [D, ISH], F32, kind="ExternalInput").ap()
    gr_d = nc.dram_tensor("gr", [IR, D], F32, kind="ExternalInput").ap()
    ur_d = nc.dram_tensor("ur", [IR, D], F32, kind="ExternalInput").ap()
    dr_d = nc.dram_tensor("dr", [D, IR], F32, kind="ExternalInput").ap()
    mask_c_d = nc.dram_tensor("mask_c", [128, 128], F32, kind="ExternalInput").ap()
    mask_e_d = nc.dram_tensor("mask_e", [128, 128], F32, kind="ExternalInput").ap()
    mask_cb_d = nc.dram_tensor("mask_cb", [128, 128], BF16, kind="ExternalInput").ap()
    mask_eb_d = nc.dram_tensor("mask_eb", [128, 128], BF16, kind="ExternalInput").ap()
    ones_bf_d = nc.dram_tensor("ones_bf", [128, 1], BF16, kind="ExternalInput").ap()
    ones_f32r_d = nc.dram_tensor("ones_f32r", [1, 128], F32, kind="ExternalInput").ap()
    out_d = nc.dram_tensor("out", [BS, D], F32, kind="ExternalOutput").ap()

    def idram(name, shape, dt=BF16):
        return nc.dram_tensor(name, shape, dt, kind="Internal").ap()

    wq_bf = idram("wq_bf", [D, D]); wk_bf = idram("wk_bf", [D, D])
    wv_bf = idram("wv_bf", [D, D]); wo_bf = idram("wo_bf", [D, D])
    r_bf = idram("r_bf", [16, D])
    gsh_bf = idram("gsh_bf", [ISH, D]); ush_bf = idram("ush_bf", [ISH, D])
    dsh_bf = idram("dsh_bf", [D, ISH])
    gr_bf = idram("gr_bf", [IR, D]); ur_bf = idram("ur_bf", [IR, D])
    dr_bf = idram("dr_bf", [D, IR])
    xn_dram = idram("xn_dram", [BS, D])
    xnT_dram = idram("xnT_dram", [D, BS])
    q_tok = idram("q_tok", [BS, D]); k_tok = idram("k_tok", [BS, D])
    v_tok = idram("v_tok", [BS, D])
    resid1 = idram("resid1", [BS, D], F32)
    h_dram = idram("h_dram", [ISH, BS])
    hr_dram = idram("hr_dram", [IR, BS])

    C = {}
    g_es = ExitStack()

    def mkctr(name):
        return Ctr(g_es.enter_context(nc.semaphore(name)))

    for nm in ["h1", "h2", "bn", "sd", "xn", "xtl", "wt", "qmm", "qev",
               "cst", "qk", "th", "mk", "ex", "pv", "dn", "rc", "bc", "rr", "pve",
               "wol", "omm", "oev", "rt", "lmm", "lcb", "e2c", "gumm", "sil",
               "hmu", "xn2", "dtl", "dmm", "dev"]:
        C[nm] = mkctr(nm)
    M = {}
    for nm, k in [("xl", 3), ("xns", 2), ("xts", 2), ("qst", 2), ("hl", 2),
                  ("xc", 3), ("oxl", 3), ("ost", 2), ("r1l", 2), ("htl", 2),
                  ("gul", 2), ("hst", 2), ("dst", 2)]:
        M[nm] = [mkctr(f"{nm}_{i}") for i in range(k)]

    mask_c_sb = g_es.enter_context(nc.sbuf_tensor("mask_c_sb", [128, 128], F32))
    mask_e_sb = g_es.enter_context(nc.sbuf_tensor("mask_e_sb", [128, 128], F32))
    mask_cb_sb = g_es.enter_context(nc.sbuf_tensor("mask_cb_sb", [128, 128], BF16))
    mask_eb_sb = g_es.enter_context(nc.sbuf_tensor("mask_eb_sb", [128, 128], BF16))
    ones_bf_sb = g_es.enter_context(nc.sbuf_tensor("ones_bf_sb", [128, 1], BF16))
    ones_f32r_sb = g_es.enter_context(nc.sbuf_tensor("ones_f32r_sb", [1, 128], F32))
    cst_sb = g_es.enter_context(nc.sbuf_tensor("cst_sb", [128, 2], F32))
    onesrow_bf = g_es.enter_context(nc.sbuf_tensor("onesrow_bf", [1, 128], BF16))
    zrow_bf = g_es.enter_context(nc.sbuf_tensor("zrow_bf", [1, 512], BF16))
    comb_sb = g_es.enter_context(nc.sbuf_tensor("comb_sb", [128, BST], F32))
    nc.const_aps.register(F32, -float(CAP), cst_sb[:, 0:1])
    nc.const_aps.register(F32, float(EPS), cst_sb[:, 1:2])

    blk_es = ExitStack()
    block = blk_es.enter_context(nc.Block())

    @block.sync
    def _(sync):
        C["cst"].dma(sync.dma_start(mask_c_sb[:], mask_c_d[:]))
        C["cst"].dma(sync.dma_start(mask_e_sb[:], mask_e_d[:]))
        C["cst"].dma(sync.dma_start(mask_cb_sb[:], mask_cb_d[:]))
        C["cst"].dma(sync.dma_start(mask_eb_sb[:], mask_eb_d[:]))
        C["cst"].dma(sync.dma_start(ones_bf_sb[:], ones_bf_d[:]))
        C["cst"].dma(sync.dma_start(ones_f32r_sb[:], ones_f32r_d[:]))

    @block.vector
    def _(v):
        v.memset(cst_sb[:, 0:1], -float(CAP))
        v.memset(cst_sb[:, 1:2], float(EPS))
        v.memset(onesrow_bf[:], 1.0)
        C["cst"].inc(v.memset(zrow_bf[:], 0.0), 1)
    CST_ALL = C["cst"].n

    hop1 = [("wv", wv_bf[:], wv_d[:]), ("wk", wk_bf[:], wk_d[:]),
            ("wq", wq_bf[:], wq_d[:]), ("wo", wo_bf[:], wo_d[:]),
            ("r", r_bf[0:NEXP, :], r_d[:]),
            ("r2", r_bf[NEXP:2 * NEXP, :], r_d[:]),
            ("gsh", gsh_bf[:], gsh_d[:]), ("ush", ush_bf[:], ush_d[:]),
            ("gr", gr_bf[:], gr_d[:]), ("ur", ur_bf[:], ur_d[:]),
            ("dsh", dsh_bf[:], dsh_d[:]), ("dr", dr_bf[:], dr_d[:])]
    H1 = {}
    nn = 0
    for nm, _dst, _src in hop1:
        nn += 16
        H1[nm] = nn
    H1["r"] = H1["r2"]
    XTS_ALL_PRE = (16 * ((HT + 1) // 2), 16 * (HT // 2))
    _nxts = B * HT
    XTS_B_PRE = (16 * ((_nxts + 1) // 2), 16 * (_nxts // 2))

    @block.gpsimd
    def _(g):
        for i, (nm, dst, srcp) in enumerate(hop1):
            if i > 0:
                g.wait_ge(C["h1"].sem, C["h1"].n)
            if nm == "wv":
                g.wait_ge(M["xts"][0].sem, XTS_ALL_PRE[0])
                g.wait_ge(M["xts"][1].sem, XTS_ALL_PRE[1])
            if nm == "gsh":
                g.wait_ge(M["xts"][0].sem, XTS_B_PRE[0])
                g.wait_ge(M["xts"][1].sem, XTS_B_PRE[1])
            C["h1"].dma(g.dma_start(dst, srcp))

    p0_es = ExitStack()
    x_sb = [p0_es.enter_context(nc.sbuf_tensor(f"x_sb{i}", [128, D], F32))
            for i in range(3)]
    xn_st = [p0_es.enter_context(nc.sbuf_tensor(f"xn_st{i}", [128, D], BF16))
             for i in range(2)]
    st6 = [p0_es.enter_context(nc.sbuf_tensor(f"st6_{i}", [128, 6 * DCK], F32))
           for i in range(2)]
    st2 = [p0_es.enter_context(nc.sbuf_tensor(f"st2_{i}", [128, 2], F32))
           for i in range(2)]
    sdv = [p0_es.enter_context(nc.sbuf_tensor(f"sdv{i}", [128, 1], F32))
           for i in range(2)]
    rstd = [p0_es.enter_context(nc.sbuf_tensor(f"rstd{i}", [128, 1], F32))
            for i in range(2)]
    xt_st = [p0_es.enter_context(nc.sbuf_tensor(f"xt_st{i}", [128, S], BF16))
             for i in range(2)]

    for b in range(B):
        @block.sync
        def _(sync, b=b):
            for st in range(ST):
                gi = b * ST + st
                if gi >= 3:
                    sync.wait_ge(C["xn"].sem, gi - 2)
                M["xl"][gi % 3].dma(sync.dma_start(
                    x_sb[gi % 3][:], x_d[gi * 128:(gi + 1) * 128, :]))
                if st >= 1:
                    sync.wait_ge(C["xn"].sem, gi)
                    M["xns"][(gi - 1) % 2].dma(sync.dma_start(
                        xn_dram[(gi - 1) * 128:gi * 128, :],
                        xn_st[(gi - 1) % 2][:]))
            sync.wait_ge(C["xn"].sem, b * ST + ST)
            gi = b * ST + ST - 1
            M["xns"][gi % 2].dma(sync.dma_start(
                xn_dram[gi * 128:(gi + 1) * 128, :], xn_st[gi % 2][:]))

        @block.vector
        def _(v, b=b):
            for st in range(ST):
                p = st % 2
                gi = b * ST + st
                v.wait_ge(M["xl"][gi % 3].sem, 16 * (gi // 3 + 1))
                if gi >= 2:
                    v.wait_ge(C["bn"].sem, 3 * (gi - 1))
                    v.wait_ge(M["xns"][gi % 2].sem, 16 * (gi // 2))
                for ck in range(DCK):
                    w = min(512, D - ck * 512)
                    vi = v.bn_stats(st6[p][:, ck * 6:(ck + 1) * 6],
                                    x_sb[gi % 3][:, ck * 512:ck * 512 + w])
                C["bn"].inc(vi, 1)
                v.wait_ge(C["bn"].sem, C["bn"].n)
                C["bn"].inc(v.bn_aggr(st2[p][:], st6[p][:]), 1)
                v.wait_ge(C["sd"].sem, gi + 1)
                C["bn"].inc(v.reciprocal(rstd[p][:], sdv[p][:]), 1)
                v.wait_ge(C["bn"].sem, C["bn"].n)
                C["xn"].inc(v.tensor_scalar(
                    xn_st[p][:], x_sb[gi % 3][:], st2[p][:, 0:1], rstd[p][:],
                    ALU.subtract, ALU.mult), 1)

        @block.scalar
        def _(s, b=b):
            if b == 0:
                s.wait_ge(C["cst"].sem, CST_ALL)
            for st in range(ST):
                p = st % 2
                gi = b * ST + st
                s.wait_ge(C["bn"].sem, 3 * gi + 2)
                C["sd"].inc(s.activation(out=sdv[p][:], in_=st2[p][:, 1:2],
                                         func=AF.Sqrt, bias=float(EPS)), 1)

        @block.sync
        def _(sync, b=b):
            sync.wait_ge(M["xns"][0].sem, 16 * ((ST * (b + 1) + 1) // 2))
            sync.wait_ge(M["xns"][1].sem, 16 * (ST * (b + 1) // 2))
            for ht in range(HT):
                p = ht % 2
                gi = b * HT + ht
                if gi >= 2:
                    sync.wait_ge(M["xts"][p].sem, 16 * (gi // 2))
                C["xtl"].dma(sync.dma_start(
                    xt_st[p][:], xn_dram[b * S:(b + 1) * S,
                                         ht * 128:(ht + 1) * 128],
                    transpose=True))
                sync.wait_ge(C["xtl"].sem, C["xtl"].n)
                M["xts"][p].dma(sync.dma_start(
                    xnT_dram[ht * 128:(ht + 1) * 128, b * S:(b + 1) * S],
                    xt_st[p][:]))
        if b == 0:
            XTS_B0 = (M["xts"][0].n, M["xts"][1].n)
    XTS_ALL = (M["xts"][0].n, M["xts"][1].n)

    def wait_xts(eng):
        eng.wait_ge(M["xts"][0].sem, XTS_ALL[0])
        eng.wait_ge(M["xts"][1].sem, XTS_ALL[1])

    p0_es.close()
    p1a_es = ExitStack()
    wt_buf = [p1a_es.enter_context(nc.sbuf_tensor(f"wt_buf{i}", [128, HT * D], BF16))
              for i in range(2)]
    xc_buf = [p1a_es.enter_context(nc.sbuf_tensor(f"xc_buf{i}", [128, HT * 128], BF16))
              for i in range(3)]
    qev = [p1a_es.enter_context(nc.sbuf_tensor(f"qev{i}", [128, D], BF16))
           for i in range(2)]
    p1a_ps = ExitStack()
    ps_q = [p1a_ps.enter_context(nc.psum_tensor(f"ps_q{i}", [128, D], F32))
            for i in range(2)]

    p1a_W = [("wv", wv_bf, v_tok), ("wk", wk_bf, k_tok), ("wq", wq_bf, q_tok)]
    NW = len(p1a_W)
    QST_B0 = []

    @block.sync
    def _(sync):
        for wi, (nm, wbf, dstd) in enumerate(p1a_W):
            wp = wi % 2
            sync.wait_ge(C["h1"].sem, H1[nm])
            if wi >= 2:
                sync.wait_ge(C["qmm"].sem, (wi - 1) * BST)
            for ht in range(HT):
                C["wt"].dma(sync.dma_start(
                    wt_buf[wp][:, ht * D:(ht + 1) * D],
                    wbf[0:D, ht * 128:(ht + 1) * 128], transpose=True))
            for st in range(BST):
                gi = wi * BST + st
                if st == 0 and wi == 0:
                    sync.wait_ge(M["xts"][0].sem, XTS_B0[0])
                    sync.wait_ge(M["xts"][1].sem, XTS_B0[1])
                if st == ST and wi == 0:
                    wait_xts(sync)
                if gi >= 3:
                    sync.wait_ge(C["qmm"].sem, gi - 2)
                M["xc"][gi % 3].dma(sync.dma_start(
                    xc_buf[gi % 3].ap().rearrange("p (ht s) -> p ht s", ht=HT),
                    xnT_dram[:, st * 128:(st + 1) * 128]
                    .rearrange("(ht p) s -> p ht s", p=128)))
                if gi >= 1:
                    sync.wait_ge(C["qev"].sem, gi)
                    pst, pw = divmod(gi - 1, BST)
                    M["qst"][(gi - 1) % 2].dma(sync.dma_start(
                        p1a_W[pst][2][pw * 128:(pw + 1) * 128, :],
                        qev[(gi - 1) % 2][:]))
                    if pst == 2 and pw == ST - 1:
                        QST_B0.append((M["qst"][0].n, M["qst"][1].n))
        sync.wait_ge(C["qev"].sem, NW * BST)
        M["qst"][(NW * BST - 1) % 2].dma(sync.dma_start(
            q_tok[(BST - 1) * 128:BST * 128, :], qev[(NW * BST - 1) % 2][:]))
        if not QST_B0:
            QST_B0.append((M["qst"][0].n, M["qst"][1].n))
    QST_ALL = (M["qst"][0].n, M["qst"][1].n)

    @block.tensor
    def _(t):
        for wi, (nm, wbf, dstd) in enumerate(p1a_W):
            wp = wi % 2
            t.wait_ge(C["wt"].sem, 16 * HT * (wi + 1))
            for st in range(BST):
                gi = wi * BST + st
                p = gi % 2
                t.wait_ge(M["xc"][gi % 3].sem, 16 * (gi // 3 + 1))
                if gi >= 2:
                    t.wait_ge(C["qev"].sem, gi - 1)
                for ht in range(HT):
                    for ck in range(DCK):
                        w = min(512, D - ck * 512)
                        mi = t.matmul(
                            ps_q[p][:, ck * 512:ck * 512 + w],
                            xc_buf[gi % 3][:, ht * 128:(ht + 1) * 128],
                            wt_buf[wp][:, ht * D + ck * 512:
                                        ht * D + ck * 512 + w],
                            start=(ht == 0), stop=(ht == HT - 1))
                C["qmm"].inc(mi, 1)

    @block.vector
    def _(v):
        for gi in range(NW * BST):
            p = gi % 2
            v.wait_ge(C["qmm"].sem, gi + 1)
            if gi >= 2:
                v.wait_ge(M["qst"][p].sem, 16 * (gi // 2))
            C["qev"].inc(v.tensor_copy(qev[p][:], ps_q[p][:]), 1)

    QMM_ALL = NW * BST
    p1a_ps.close()
    p1a_es.close()

    def chunk_jts(ci):
        c0, c1 = ci * CPT, ci * CPT + CPT - 1
        res = []
        for jt in range(max(0, c0 - WT), min(ST - 1, c1) + 1):
            lo, hi = max(c0, jt), min(c1, jt + WT, ST - 1)
            if lo <= hi:
                res.append((jt, lo, hi))
        return res

    for b in range(B):
        atk_es = ExitStack()
        attnT_sb = atk_es.enter_context(
            nc.sbuf_tensor(f"attnT{b}", [128, H * S], BF16))
        att_es = ExitStack()
        qt_sb = [att_es.enter_context(nc.sbuf_tensor(f"qt{b}_{i}", [128, S], BF16))
                 for i in range(2)]
        kt_sb = [att_es.enter_context(nc.sbuf_tensor(f"kt{b}_{i}", [128, S], BF16))
                 for i in range(2)]
        v_sb = [att_es.enter_context(nc.sbuf_tensor(f"v{b}_{i}", [128, ST * DH], BF16))
                for i in range(2)]
        pt_sb = [att_es.enter_context(
            nc.sbuf_tensor(f"pt{b}_{i}", [128, ST * SLOTW], BF16)) for i in range(2)]
        t_sb = [att_es.enter_context(nc.sbuf_tensor(f"t{b}_{i}", [128, SLOTW], F32))
                for i in range(2)]
        rec_sb = att_es.enter_context(nc.sbuf_tensor(f"rec{b}", [1, S], F32))
        rrep_sb = [att_es.enter_context(nc.sbuf_tensor(f"rrep{b}_{i}", [128, CW], F32))
                   for i in range(4)]
        att_ps = ExitStack()
        ps_sT = [att_ps.enter_context(nc.psum_tensor(f"ps_sT{b}_{i}", [128, 1024], F32))
                 for i in range(2)]
        ps_tl = att_ps.enter_context(nc.psum_tensor(f"ps_tl{b}", [128, 256], F32))
        ps_pv = [att_ps.enter_context(nc.psum_tensor(f"ps_pv{b}_{i}", [128, 512], F32))
                 for i in range(2)]
        ps_db = att_ps.enter_context(nc.psum_tensor(f"ps_db{b}", [128, 512], F32))

        @block.sync
        def _(sync, b=b):
            if b == 0:
                sync.wait_ge(M["qst"][0].sem, QST_B0[0][0])
                sync.wait_ge(M["qst"][1].sem, QST_B0[0][1])
            else:
                sync.wait_ge(M["ost"][0].sem, 16 * ((ST * b + 1) // 2))
                sync.wait_ge(M["ost"][1].sem, 16 * (ST * b // 2))
                sync.wait_ge(C["oev"].sem, ST * b)
                sync.wait_ge(C["omm"].sem, ST * b)
            for hd in range(H):
                p = hd % 2
                gi = b * H + hd
                if gi >= 2:
                    sync.wait_ge(C["pve"].sem, (gi - 1) * NCH)
                for src, dstb in ((q_tok, qt_sb), (k_tok, kt_sb)):
                    M["hl"][p].dma(sync.dma_start(
                        dstb[p][:],
                        src[b * S:(b + 1) * S, hd * DH:(hd + 1) * DH],
                        transpose=True))
                M["hl"][p].dma(sync.dma_start(
                    v_sb[p].ap().rearrange("p (jt d) -> p jt d", jt=ST),
                    v_tok[b * S:(b + 1) * S, hd * DH:(hd + 1) * DH]
                    .rearrange("(jt p) d -> p jt d", p=128)))

        @block.tensor
        def _(t, b=b):
            if b == 0:
                t.wait_ge(C["cst"].sem, CST_ALL)
                t.wait_ge(C["qev"].sem, QMM_ALL)
            else:
                t.wait_ge(C["oev"].sem, ST * b)
                t.wait_ge(M["ost"][0].sem, 16 * ((ST * b + 1) // 2))
                t.wait_ge(M["ost"][1].sem, 16 * (ST * b // 2))
            def emit_qk_jt(hd, jt):
                p = hd % 2
                gi = b * H + hd
                nit = NIT(jt)
                wlen = nit * 128
                tj = gi * ST + jt
                if tj >= 2:
                    t.wait_ge(C["th"].sem, tj - 1)
                wmain = min(wlen, 1024)
                for ck in range((wmain + 511) // 512):
                    w = min(512, wmain - ck * 512)
                    mi = t.matmul(
                        ps_sT[tj % 2][:, ck * 512:ck * 512 + w],
                        kt_sb[p][:, jt * 128:(jt + 1) * 128],
                        qt_sb[p][:, jt * 128 + ck * 512:
                                  jt * 128 + ck * 512 + w],
                        start=True, stop=True)
                if wlen > 1024:
                    mi = t.matmul(
                        ps_tl[:, (tj % 2) * 128:(tj % 2) * 128 + (wlen - 1024)],
                        kt_sb[p][:, jt * 128:(jt + 1) * 128],
                        qt_sb[p][:, jt * 128 + 1024:jt * 128 + wlen],
                        start=True, stop=True)
                C["qk"].inc(mi, 1)

            def emit_den(hd, ci):
                p = hd % 2
                gi = b * H + hd
                jts = chunk_jts(ci)
                k = gi * NCH + ci
                t.wait_ge(C["mk"].sem,
                          mk_at[(b, hd, max(j for j, _, _ in jts))])
                if k >= 1:
                    t.wait_ge(C["rr"].sem, k)
                t.matmul(ps_db[0:1, 0:CW], onesrow_bf[0:1, 0:1],
                         zrow_bf[0:1, 0:CW], start=True, stop=False)
                for idx, (jt, lo, hi) in enumerate(jts):
                    w = (hi - lo + 1) * 128
                    mi2 = t.matmul(
                        ps_db[0:1, (lo - ci * CPT) * 128:
                                   (lo - ci * CPT) * 128 + w],
                        ones_bf_sb[:],
                        pt_sb[p][:, jt * SLOTW + (lo - jt) * 128:
                                  jt * SLOTW + (lo - jt) * 128 + w],
                        start=False, stop=(idx == len(jts) - 1))
                C["dn"].inc(mi2, 1)

            def emit_bc(hd, ci):
                gi = b * H + hd
                k = gi * NCH + ci
                t.wait_ge(C["rc"].sem, k + 1)
                C["bc"].inc(t.matmul(
                    ps_db[:, 0:CW], ones_f32r_sb[:],
                    rec_sb[0:1, ci * CW:(ci + 1) * CW],
                    start=True, stop=True), 1)

            def emit_pv(hd, ci):
                p = hd % 2
                gi = b * H + hd
                jts = chunk_jts(ci)
                k = gi * NCH + ci
                if k >= 2:
                    t.wait_ge(C["pve"].sem, k - 1)
                t.matmul(ps_pv[k % 2][:, 0:CW], onesrow_bf[:],
                         zrow_bf[0:1, 0:CW], start=True, stop=False)
                mi = None
                for idx, (jt, lo, hi) in enumerate(jts):
                    w = (hi - lo + 1) * 128
                    mi = t.matmul(
                        ps_pv[k % 2][:, (lo - ci * CPT) * 128:
                                     (lo - ci * CPT) * 128 + w],
                        v_sb[p][:, jt * DH:(jt + 1) * DH],
                        pt_sb[p][:, jt * SLOTW + (lo - jt) * 128:
                                  jt * SLOTW + (lo - jt) * 128 + w],
                        start=False, stop=(idx == len(jts) - 1))
                C["pv"].inc(mi, 1)

            for hd in range(H):
                p = hd % 2
                gi = b * H + hd
                t.wait_ge(M["hl"][p].sem, 48 * (gi // 2 + 1))
                for jt in range(ST):
                    emit_qk_jt(hd, jt)
                    if (jt + 1) % CPT == 0:
                        ci = (jt + 1) // CPT - 1
                        if hd >= 1:
                            emit_pv(hd - 1, ci)
                        if ci >= 1:
                            emit_bc(hd, ci - 1)
                        emit_den(hd, ci)
                emit_bc(hd, NCH - 1)
            for ci in range(NCH):
                emit_pv(H - 1, ci)

        @block.scalar
        def _(s, b=b):
            if b:
                s.wait_ge(M["ost"][0].sem, 16 * ((ST * b + 1) // 2))
                s.wait_ge(M["ost"][1].sem, 16 * (ST * b // 2))
            for hd in range(H):
                p = hd % 2
                gi = b * H + hd
                if gi >= 2:
                    s.wait_ge(C["pve"].sem, (gi - 1) * NCH)
                for jt in range(ST):
                    nit = NIT(jt)
                    tj = gi * ST + jt
                    s.wait_ge(C["qk"].sem, tj + 1)
                    if tj >= 2:
                        s.wait_ge(C["ex"].sem, tj - 1)
                    wlen = nit * 128
                    wmain = min(wlen, 1024)
                    ti = s.activation(
                        out=t_sb[tj % 2][:, 0:wmain],
                        in_=ps_sT[tj % 2][:, 0:wmain], func=AF.Tanh,
                        scale=float(SCALE / CAP))
                    if wlen > 1024:
                        ti = s.activation(
                            out=t_sb[tj % 2][:, 1024:wlen],
                            in_=ps_tl[:, (tj % 2) * 128:
                                      (tj % 2) * 128 + (wlen - 1024)],
                            func=AF.Tanh, scale=float(SCALE / CAP))
                    C["th"].inc(ti, 1)
                    s.wait_ge(C["th"].sem, C["th"].n)
                    C["ex"].inc(s.activation(
                        out=pt_sb[p][:, jt * SLOTW:jt * SLOTW + nit * 128],
                        in_=t_sb[tj % 2][:, 0:nit * 128], func=AF.Exp,
                        scale=float(CAP), bias=-float(CAP)), 1)

        @block.gpsimd
        def _(g, b=b):
            for hd in range(H):
                p = hd % 2
                gi = b * H + hd
                for jt in range(ST):
                    nit = NIT(jt)
                    tj = gi * ST + jt
                    g.wait_ge(C["ex"].sem, tj + 1)
                    sl = pt_sb[p][:, jt * SLOTW:jt * SLOTW + 128]
                    mi = g.tensor_mul(sl, sl, mask_cb_sb[:])
                    C["mk"].inc(mi, 1)
                    if nit == WT + 1:
                        sl = pt_sb[p][:, jt * SLOTW + WT * 128:
                                      jt * SLOTW + (WT + 1) * 128]
                        mi = g.tensor_mul(sl, sl, mask_eb_sb[:])
                        C["mk"].inc(mi, 1)
                    assert C["mk"].n == mk_at[(b, hd, jt)]

        @block.vector
        def _(v, b=b):
            for hd in range(H):
                gi = b * H + hd
                for ci in range(NCH):
                    k = gi * NCH + ci
                    if hd >= 1:
                        kp = (gi - 1) * NCH + ci
                        v.wait_ge(C["pv"].sem, kp + 1)
                        v.wait_ge(C["rr"].sem, kp + 1)
                        C["pve"].inc(v.tensor_tensor(
                            attnT_sb[:, (hd - 1) * S + ci * CW:
                                     (hd - 1) * S + (ci + 1) * CW],
                            ps_pv[kp % 2][:, 0:CW], rrep_sb[kp % 4][:, 0:CW],
                            ALU.mult), 1)
                    v.wait_ge(C["dn"].sem, k + 1)
                    if k >= NCH:
                        v.wait_ge(C["bc"].sem, k - NCH + 1)
                    C["rc"].inc(v.reciprocal(
                        rec_sb[0:1, ci * CW:(ci + 1) * CW],
                        ps_db[0:1, 0:CW]), 1)
                    v.wait_ge(C["bc"].sem, k + 1)
                    if k >= 4:
                        v.wait_ge(C["pve"].sem, k - 3)
                    C["rr"].inc(v.tensor_copy(rrep_sb[k % 4][:, 0:CW],
                                              ps_db[:, 0:CW]), 1)
            for ci in range(NCH):
                kp = (b * H + H - 1) * NCH + ci
                v.wait_ge(C["pv"].sem, kp + 1)
                v.wait_ge(C["rr"].sem, kp + 1)
                C["pve"].inc(v.tensor_tensor(
                    attnT_sb[:, (H - 1) * S + ci * CW:
                             (H - 1) * S + (ci + 1) * CW],
                    ps_pv[kp % 2][:, 0:CW], rrep_sb[kp % 4][:, 0:CW],
                    ALU.mult), 1)

        att_ps.close()
        att_es.close()
        oc_es = ExitStack()
        wo_sb = oc_es.enter_context(nc.sbuf_tensor(f"wo_sb{b}", [128, HT * D], BF16))
        ox_sb = [oc_es.enter_context(nc.sbuf_tensor(f"ox{b}_{i}", [128, D], F32))
                 for i in range(3)]
        oev = [oc_es.enter_context(nc.sbuf_tensor(f"oev{b}_{i}", [128, D], F32))
               for i in range(2)]
        oc_ps = ExitStack()
        ps_o = [oc_ps.enter_context(nc.psum_tensor(f"ps_o{b}_{i}", [128, D], F32))
                for i in range(2)]

        @block.sync
        def _(sync, b=b):
            sync.wait_ge(C["h1"].sem, H1["wo"])
            sync.wait_ge(C["pve"].sem, (b + 1) * H * NCH)
            for ht in range(HT):
                C["wol"].dma(sync.dma_start(
                    wo_sb[:, ht * D:(ht + 1) * D],
                    wo_bf[0:D, ht * 128:(ht + 1) * 128], transpose=True))

        @block.scalar
        def _(s, b=b):
            s.wait_ge(C["pve"].sem, (b + 1) * H * NCH)
            for st in range(ST):
                gi = b * ST + st
                p3 = gi % 3
                if st >= 3:
                    s.wait_ge(C["oev"].sem, gi - 2)
                M["oxl"][p3].dma(s.dma_start(
                    ox_sb[p3][:], x_d[gi * 128:(gi + 1) * 128, :]))
                if st >= 2:
                    s.wait_ge(C["oev"].sem, gi - 1)
                    M["ost"][(gi - 2) % 2].dma(s.dma_start(
                        resid1[(gi - 2) * 128:(gi - 1) * 128, :],
                        oev[(gi - 2) % 2][:]))
            s.wait_ge(C["oev"].sem, (b + 1) * ST)
            for st in (ST - 2, ST - 1):
                gi = b * ST + st
                M["ost"][gi % 2].dma(s.dma_start(
                    resid1[gi * 128:(gi + 1) * 128, :], oev[gi % 2][:]))

        @block.tensor
        def _(t, b=b):
            t.wait_ge(C["wol"].sem, 16 * HT * (b + 1))
            t.wait_ge(C["pve"].sem, (b + 1) * H * NCH)
            for st in range(ST):
                p = st % 2
                gi = b * ST + st
                if st >= 2:
                    t.wait_ge(C["oev"].sem, gi - 1)
                for ht in range(HT):
                    for ck in range(DCK):
                        w = min(512, D - ck * 512)
                        mi = t.matmul(
                            ps_o[p][:, ck * 512:ck * 512 + w],
                            attnT_sb[:, ht * S + st * 128:ht * S + (st + 1) * 128],
                            wo_sb[:, ht * D + ck * 512:ht * D + ck * 512 + w],
                            start=(ht == 0), stop=(ht == HT - 1))
                C["omm"].inc(mi, 1)

        @block.vector
        def _(v, b=b):
            for st in range(ST):
                p = st % 2
                gi = b * ST + st
                v.wait_ge(C["omm"].sem, gi + 1)
                v.wait_ge(M["oxl"][gi % 3].sem, 16 * (gi // 3 + 1))
                if st >= 2:
                    v.wait_ge(M["ost"][p].sem, 16 * (gi // 2))
                C["oev"].inc(v.tensor_add(oev[p][:], ps_o[p][:], ox_sb[gi % 3][:]), 1)

        oc_ps.close()
        oc_es.close()
        atk_es.close()

    OST_ALL = (M["ost"][0].n, M["ost"][1].n)
    OEV_ALL = C["oev"].n

    def wait_ost(eng):
        eng.wait_ge(M["ost"][0].sem, OST_ALL[0])
        eng.wait_ge(M["ost"][1].sem, OST_ALL[1])

    p2a_es = ExitStack()
    xnT_sb = p2a_es.enter_context(nc.sbuf_tensor("xnT_sb", [128, HT * BS], BF16))
    rt_sb = p2a_es.enter_context(nc.sbuf_tensor("rt_sb", [128, HT * 16], BF16))
    lg_sb = [p2a_es.enter_context(nc.sbuf_tensor(f"lg{i}", [128, NEXP], F32))
             for i in range(2)]
    m1_sb = p2a_es.enter_context(nc.sbuf_tensor("m1_sb", [128, 1], F32))
    eq_sb = p2a_es.enter_context(nc.sbuf_tensor("eq_sb", [128, NEXP], F32))
    m2_sb = p2a_es.enter_context(nc.sbuf_tensor("m2_sb", [128, 1], F32))
    e2_sb = p2a_es.enter_context(nc.sbuf_tensor("e2_sb", [128, 1], F32))
    dn_sb = p2a_es.enter_context(nc.sbuf_tensor("dn_sb", [128, 1], F32))
    rc2_sb = p2a_es.enter_context(nc.sbuf_tensor("rc2_sb", [128, 1], F32))
    gt_buf = [p2a_es.enter_context(nc.sbuf_tensor(f"gt_buf{i}", [128, HT * SUP], BF16))
              for i in range(2)]
    ut_buf = [p2a_es.enter_context(nc.sbuf_tensor(f"ut_buf{i}", [128, HT * SUP], BF16))
              for i in range(2)]
    sil_sb = [p2a_es.enter_context(nc.sbuf_tensor(f"sil{i}", [128, TC], F32))
              for i in range(2)]
    h_st = [p2a_es.enter_context(nc.sbuf_tensor(f"h_st{i}", [128, TC], BF16))
            for i in range(2)]
    p2a_ps1 = ExitStack()
    ps_l = [p2a_ps1.enter_context(nc.psum_tensor(f"ps_l{i}", [128, 512], F32))
            for i in range(2)]

    p2a_cfgs = [("sh", gsh_bf, ush_bf, h_dram, ISH),
                ("r", gr_bf, ur_bf, hr_dram, IR)]
    LCB_ALL = 10 * BST

    @block.sync
    def _(sync):
        wait_ost(sync)
        sync.wait_ge(C["oev"].sem, OEV_ALL)
        C["xn2"].dma(sync.dma_start(
            xnT_sb.ap().rearrange("p (ht s) -> p ht s", ht=HT),
            xnT_dram.rearrange("(ht p) s -> p ht s", p=128)))
        for ht in range(HT):
            C["rt"].dma(sync.dma_start(
                rt_sb[:, ht * 16:(ht + 1) * 16],
                r_bf[0:16, ht * 128:(ht + 1) * 128], transpose=True))
        for seg, (nm, gtc, utc, hdst, II) in enumerate(p2a_cfgs):
            sync.wait_ge(C["h1"].sem, H1["gsh" if nm == "sh" else "gr"])
            sync.wait_ge(C["h1"].sem, H1["ush" if nm == "sh" else "ur"])
            base = 0 if nm == "sh" else nsup_sh
            for sup in range(II // SUP):
                gsup = base + sup
                p = gsup % 2
                if gsup >= 2:
                    sync.wait_ge(C["gumm"].sem, gu_done[gsup - 2])
                for ht in range(HT):
                    M["gul"][p].dma(sync.dma_start(
                        gt_buf[p][:, ht * SUP:(ht + 1) * SUP],
                        gtc[sup * SUP:(sup + 1) * SUP,
                            ht * 128:(ht + 1) * 128], transpose=True))
                    M["gul"][p].dma(sync.dma_start(
                        ut_buf[p][:, ht * SUP:(ht + 1) * SUP],
                        utc[sup * SUP:(sup + 1) * SUP,
                            ht * 128:(ht + 1) * 128], transpose=True))

    @block.tensor
    def _(t):
        t.wait_ge(C["xn2"].sem, 16)
        t.wait_ge(C["rt"].sem, 16 * HT)
        t.wait_ge(C["oev"].sem, OEV_ALL)
        wait_ost(t)
        for st in range(BST):
            p = st % 2
            if st >= 2:
                t.wait_ge(C["lcb"].sem, 10 * (st - 1))
            for ht in range(HT):
                mi = t.matmul(ps_l[p][:, 0:NEXP],
                              xnT_sb[:, ht * BS + st * 128:
                                     ht * BS + (st + 1) * 128],
                              rt_sb[:, ht * 16:ht * 16 + NEXP],
                              start=(ht == 0), stop=(ht == HT - 1))
            C["lmm"].inc(mi, 1)

    @block.vector
    def _(v):
        wait_ost(v)
        for st in range(BST):
            p = st % 2
            v.wait_ge(C["lmm"].sem, st + 1)
            if st >= 1:
                v.wait_ge(C["e2c"].sem, st)
            C["lcb"].inc(v.tensor_copy(lg_sb[p][:], ps_l[p][:, 0:NEXP]), 1)
            v.wait_ge(C["lcb"].sem, C["lcb"].n)
            C["lcb"].inc(v.reduce_max(m1_sb[:], lg_sb[p][:], axis=AX), 1)
            v.wait_ge(C["lcb"].sem, C["lcb"].n)
            C["lcb"].inc(v.tensor_scalar(eq_sb[:], lg_sb[p][:], m1_sb[:], None,
                                         ALU.is_equal), 1)
            v.wait_ge(C["lcb"].sem, C["lcb"].n)
            C["lcb"].inc(v.scalar_tensor_tensor(eq_sb[:], eq_sb[:], -1e30,
                                                lg_sb[p][:], ALU.mult, ALU.add), 1)
            v.wait_ge(C["lcb"].sem, C["lcb"].n)
            C["lcb"].inc(v.reduce_max(m2_sb[:], eq_sb[:], axis=AX), 1)
            v.wait_ge(C["lcb"].sem, C["lcb"].n)
            C["lcb"].inc(v.tensor_sub(m2_sb[:], m2_sb[:], m1_sb[:]), 1)
            v.wait_ge(C["e2c"].sem, st + 1)
            C["lcb"].inc(v.tensor_scalar(dn_sb[:], e2_sb[:], 1.0, None, ALU.add), 1)
            v.wait_ge(C["lcb"].sem, C["lcb"].n)
            C["lcb"].inc(v.reciprocal(rc2_sb[:], dn_sb[:]), 1)
            v.wait_ge(C["lcb"].sem, C["lcb"].n)
            C["lcb"].inc(v.scalar_tensor_tensor(comb_sb[:, st:st + 1], e2_sb[:],
                                                1.0, rc2_sb[:], ALU.mult,
                                                ALU.mult), 1)
            v.wait_ge(C["lcb"].sem, C["lcb"].n)
            C["lcb"].inc(v.tensor_add(comb_sb[:, st:st + 1],
                                      comb_sb[:, st:st + 1], rc2_sb[:]), 1)

    @block.scalar
    def _(s):
        wait_ost(s)
        for st in range(BST):
            s.wait_ge(C["lcb"].sem, 6 + 10 * st)
            C["e2c"].inc(s.activation(out=e2_sb[:], in_=m2_sb[:], func=AF.Exp), 1)

    p2a_ps1.close()
    p2a_ps2 = ExitStack()
    ps_g = [p2a_ps2.enter_context(nc.psum_tensor(f"ps_g{i}", [128, TC], F32))
            for i in range(2)]
    ps_u = [p2a_ps2.enter_context(nc.psum_tensor(f"ps_u{i}", [128, TC], F32))
            for i in range(2)]

    @block.tensor
    def _(t):
        t.wait_ge(C["lcb"].sem, LCB_ALL)
        gr_i = 0
        for nm, gtc, utc, hdst, II in p2a_cfgs:
            base = 0 if nm == "sh" else nsup_sh
            for sup in range(II // SUP):
                gsup = base + sup
                p = gsup % 2
                t.wait_ge(M["gul"][p].sem, 16 * 2 * HT * (gsup // 2 + 1))
                for itl in range(SUP // 128):
                    for tc in range(NTC):
                        gp = gr_i % 2
                        if gr_i >= 2:
                            t.wait_ge(C["hmu"].sem, gr_i - 1)
                        for ht in range(HT):
                            for ck in range(TCK):
                                w = min(512, TC - ck * 512)
                                mi = t.matmul(
                                    ps_g[gp][:, ck * 512:ck * 512 + w],
                                    gt_buf[p][:, ht * SUP + itl * 128:
                                              ht * SUP + (itl + 1) * 128],
                                    xnT_sb[:, ht * BS + tc * TC + ck * 512:
                                           ht * BS + tc * TC + ck * 512 + w],
                                    start=(ht == 0), stop=(ht == HT - 1))
                        C["gumm"].inc(mi, 1)
                        for ht in range(HT):
                            for ck in range(TCK):
                                w = min(512, TC - ck * 512)
                                mi = t.matmul(
                                    ps_u[gp][:, ck * 512:ck * 512 + w],
                                    ut_buf[p][:, ht * SUP + itl * 128:
                                              ht * SUP + (itl + 1) * 128],
                                    xnT_sb[:, ht * BS + tc * TC + ck * 512:
                                           ht * BS + tc * TC + ck * 512 + w],
                                    start=(ht == 0), stop=(ht == HT - 1))
                        C["gumm"].inc(mi, 1)
                        gr_i += 1

    @block.scalar
    def _(s):
        gr_i = 0
        for nm, gtc, utc, hdst, II in p2a_cfgs:
            for sup in range(II // SUP):
                for itl in range(SUP // 128):
                    for tc in range(NTC):
                        gp = gr_i % 2
                        s.wait_ge(C["gumm"].sem, 2 * gr_i + 1)
                        if gr_i >= 2:
                            s.wait_ge(C["hmu"].sem, gr_i - 1)
                        C["sil"].inc(s.activation(out=sil_sb[gp][:],
                                                  in_=ps_g[gp][:],
                                                  func=AF.Silu), 1)
                        gr_i += 1

    @block.vector
    def _(v):
        gr_i = 0
        for nm, gtc, utc, hdst, II in p2a_cfgs:
            for sup in range(II // SUP):
                for itl in range(SUP // 128):
                    for tc in range(NTC):
                        gp = gr_i % 2
                        v.wait_ge(C["sil"].sem, gr_i + 1)
                        v.wait_ge(C["gumm"].sem, 2 * gr_i + 2)
                        if gr_i >= 2:
                            v.wait_ge(M["hst"][gp].sem, 16 * (gr_i // 2))
                        C["hmu"].inc(v.tensor_tensor(h_st[gp][:], sil_sb[gp][:],
                                                     ps_u[gp][:], ALU.mult), 1)
                        gr_i += 1

    @block.gpsimd
    def _(g):
        gr_i = 0
        for nm, gtc, utc, hdst, II in p2a_cfgs:
            for sup in range(II // SUP):
                for itl in range(SUP // 128):
                    ig = sup * (SUP // 128) + itl
                    for tc in range(NTC):
                        gp = gr_i % 2
                        g.wait_ge(C["hmu"].sem, gr_i + 1)
                        M["hst"][gp].dma(g.dma_start(
                            hdst[ig * 128:(ig + 1) * 128,
                                 tc * TC:(tc + 1) * TC], h_st[gp][:]))
                        gr_i += 1
    HST_ALL = (M["hst"][0].n, M["hst"][1].n)
    HMU_ALL = C["hmu"].n

    p2a_ps2.close()
    p2a_es.close()
    p2b_es = ExitStack()
    dt_sb = p2b_es.enter_context(nc.sbuf_tensor("dt_sb", [128, IT * OH], BF16))
    rdt_sb = p2b_es.enter_context(nc.sbuf_tensor("rdt_sb", [128, IRT * OH], BF16))
    htc = [p2b_es.enter_context(nc.sbuf_tensor(f"htc{i}", [128, IT * 128], BF16))
           for i in range(2)]
    hrc = [p2b_es.enter_context(nc.sbuf_tensor(f"hrc{i}", [128, IRT * 128], BF16))
           for i in range(2)]
    r1_sb = [p2b_es.enter_context(nc.sbuf_tensor(f"r1_{i}", [128, OH], F32))
             for i in range(2)]
    tmp_sb = [p2b_es.enter_context(nc.sbuf_tensor(f"tmp{i}", [128, OH], F32))
              for i in range(2)]
    o_st = [p2b_es.enter_context(nc.sbuf_tensor(f"o_st{i}", [128, OH], F32))
            for i in range(2)]
    p2b_ps = ExitStack()
    ps_sh = [p2b_ps.enter_context(nc.psum_tensor(f"ps_sh{i}", [128, OH], F32))
             for i in range(2)]
    ps_r = [p2b_ps.enter_context(nc.psum_tensor(f"ps_r{i}", [128, OH], F32))
            for i in range(2)]

    for oh in range(2):
        @block.sync
        def _(sync, oh=oh):
            if oh == 0:
                sync.wait_ge(C["h1"].sem, H1["dr"])
                sync.wait_ge(C["hmu"].sem, HMU_ALL)
                sync.wait_ge(M["hst"][0].sem, HST_ALL[0])
                sync.wait_ge(M["hst"][1].sem, HST_ALL[1])
            else:
                sync.wait_ge(C["dmm"].sem, 2 * BST)
            for it in range(IT):
                C["dtl"].dma(sync.dma_start(
                    dt_sb[:, it * OH:(it + 1) * OH],
                    dsh_bf[oh * OH:(oh + 1) * OH,
                           it * 128:(it + 1) * 128], transpose=True))
            for it in range(IRT):
                C["dtl"].dma(sync.dma_start(
                    rdt_sb[:, it * OH:(it + 1) * OH],
                    dr_bf[oh * OH:(oh + 1) * OH,
                          it * 128:(it + 1) * 128], transpose=True))
            for st in range(BST):
                gi = oh * BST + st
                p2 = gi % 2
                if gi >= 2:
                    sync.wait_ge(C["dmm"].sem, 2 * (gi - 1))
                M["htl"][p2].dma(sync.dma_start(
                    htc[p2].ap().rearrange("p (it s) -> p it s", it=IT),
                    h_dram[:, st * 128:(st + 1) * 128]
                    .rearrange("(it p) s -> p it s", p=128)))
                M["htl"][p2].dma(sync.dma_start(
                    hrc[p2].ap().rearrange("p (it s) -> p it s", it=IRT),
                    hr_dram[:, st * 128:(st + 1) * 128]
                    .rearrange("(it p) s -> p it s", p=128)))

        @block.scalar
        def _(s, oh=oh):
            if oh == 0:
                s.wait_ge(C["hmu"].sem, HMU_ALL)
                s.wait_ge(M["hst"][0].sem, HST_ALL[0])
                s.wait_ge(M["hst"][1].sem, HST_ALL[1])
            for st in range(BST):
                gi = oh * BST + st
                p2 = gi % 2
                if gi >= 2:
                    s.wait_ge(C["dev"].sem, 2 * (gi - 1))
                M["r1l"][p2].dma(s.dma_start(
                    r1_sb[p2][:], resid1[st * 128:(st + 1) * 128,
                                         oh * OH:(oh + 1) * OH]))
                if gi >= 2:
                    s.wait_ge(C["dev"].sem, 2 * (gi - 1))
                    pgi = gi - 2
                    pst, poh = pgi % BST, pgi // BST
                    M["dst"][pgi % 2].dma(s.dma_start(
                        out_d[pst * 128:(pst + 1) * 128,
                              poh * OH:(poh + 1) * OH], o_st[pgi % 2][:]))
            if oh == 1:
                s.wait_ge(C["dev"].sem, 4 * BST)
                for pgi in (2 * BST - 2, 2 * BST - 1):
                    pst, poh = pgi % BST, pgi // BST
                    M["dst"][pgi % 2].dma(s.dma_start(
                        out_d[pst * 128:(pst + 1) * 128,
                              poh * OH:(poh + 1) * OH], o_st[pgi % 2][:]))

        @block.tensor
        def _(t, oh=oh):
            if oh == 0:
                t.wait_ge(C["hmu"].sem, HMU_ALL)
            t.wait_ge(C["dtl"].sem, 16 * (IT + IRT) * (oh + 1))
            for st in range(BST):
                gi = oh * BST + st
                gp = gi % 2
                t.wait_ge(M["htl"][gp].sem, 32 * (gi // 2 + 1))
                if gi >= 2:
                    t.wait_ge(C["dev"].sem, 2 * (gi - 1))
                for it in range(IT):
                    for ck in range(OCK):
                        w = min(512, OH - ck * 512)
                        mi = t.matmul(
                            ps_sh[gp][:, ck * 512:ck * 512 + w],
                            htc[gp][:, it * 128:(it + 1) * 128],
                            dt_sb[:, it * OH + ck * 512:it * OH + ck * 512 + w],
                            start=(it == 0), stop=(it == IT - 1))
                C["dmm"].inc(mi, 1)
                for it in range(IRT):
                    for ck in range(OCK):
                        w = min(512, OH - ck * 512)
                        mi = t.matmul(
                            ps_r[gp][:, ck * 512:ck * 512 + w],
                            hrc[gp][:, it * 128:(it + 1) * 128],
                            rdt_sb[:, it * OH + ck * 512:it * OH + ck * 512 + w],
                            start=(it == 0), stop=(it == IRT - 1))
                C["dmm"].inc(mi, 1)

        @block.vector
        def _(v, oh=oh):
            for st in range(BST):
                gi = oh * BST + st
                gp = gi % 2
                v.wait_ge(C["dmm"].sem, 2 * (gi + 1))
                v.wait_ge(M["r1l"][gp].sem, 16 * (gi // 2 + 1))
                if gi >= 2:
                    v.wait_ge(M["dst"][gp].sem, 16 * (gi // 2))
                    v.wait_ge(C["dev"].sem, 2 * (gi - 1))
                C["dev"].inc(v.scalar_tensor_tensor(
                    tmp_sb[gp][:], ps_r[gp][:], comb_sb[:, st:st + 1],
                    r1_sb[gp][:], ALU.mult, ALU.add), 1)
                v.wait_ge(C["dev"].sem, C["dev"].n)
                C["dev"].inc(v.tensor_add(o_st[gp][:], tmp_sb[gp][:],
                                          ps_sh[gp][:]), 1)

    @block.sync
    def _(sync):
        sync.wait_ge(M["dst"][0].sem, M["dst"][0].n)
        sync.wait_ge(M["dst"][1].sem, M["dst"][1].n)

    p2b_ps.close()
    p2b_es.close()
    blk_es.close()
    g_es.close()
    return nc


def kernel(L0_attn_x, L0_attn_Wq, L0_attn_Wk, L0_attn_Wv, L0_attn_Wo,
           L0_ffn_router_weight, L0_ffn_sh_gate, L0_ffn_sh_up, L0_ffn_sh_down,
           L0_ffn_r_gate, L0_ffn_r_up, L0_ffn_r_down):
    cfg = FULL
    nc = build(cfg)
    cn = host_consts(cfg)
    x = np.asarray(L0_attn_x, dtype=np.float32)
    Bfull = x.shape[0]
    ncore = 8
    bpc = Bfull // ncore
    BS = bpc * cfg["S"]
    shared = {
        "wq": np.asarray(L0_attn_Wq, np.float32),
        "wk": np.asarray(L0_attn_Wk, np.float32),
        "wv": np.asarray(L0_attn_Wv, np.float32),
        "wo": np.asarray(L0_attn_Wo, np.float32),
        "router": np.asarray(L0_ffn_router_weight, np.float32),
        "gsh": np.asarray(L0_ffn_sh_gate, np.float32),
        "ush": np.asarray(L0_ffn_sh_up, np.float32),
        "dsh": np.asarray(L0_ffn_sh_down, np.float32),
        "gr": np.asarray(L0_ffn_r_gate, np.float32),
        "ur": np.asarray(L0_ffn_r_up, np.float32),
        "dr": np.asarray(L0_ffn_r_down, np.float32),
        "mask_c": cn["mask_c"], "mask_e": cn["mask_e"],
        "mask_cb": cn["mask_cb"], "mask_eb": cn["mask_eb"],
        "ones_bf": cn["ones_bf"], "ones_f32r": cn["ones_f32r"],
    }
    in_maps = []
    for c in range(ncore):
        m = dict(shared)
        m["x"] = np.ascontiguousarray(
            x[c * bpc:(c + 1) * bpc].reshape(BS, cfg["D"]))
        in_maps.append(m)
    res = run_bass_kernel_spmd(nc, in_maps, core_ids=list(range(ncore)))
    outs = [np.asarray(r["out"]).reshape(bpc, cfg["S"], cfg["D"])
            for r in res.results]
    return np.concatenate(outs, axis=0)


# version 13
# speedup vs baseline: 1.1901x; 1.1901x over previous
import math
import numpy as np
import ml_dtypes
from contextlib import ExitStack

import concourse.bass as bass
import concourse.mybir as mybir
from concourse.bass_utils import run_bass_kernel_spmd

F32 = mybir.dt.float32
BF16 = mybir.dt.bfloat16
AF = mybir.ActivationFunctionType
ALU = mybir.AluOpType
AX = mybir.AxisListType.X

FULL = dict(B=2, S=2048, D=2048, H=16, DH=128, WIN=1024, CAP=30.0,
            ISH=8192, IR=1024, NEXP=8, EPS=1e-5)
SMALL = dict(B=1, S=512, D=256, H=2, DH=128, WIN=256, CAP=30.0,
             ISH=512, IR=256, NEXP=8, EPS=1e-5)
MED = dict(B=1, S=2048, D=256, H=2, DH=128, WIN=1024, CAP=30.0,
           ISH=512, IR=256, NEXP=8, EPS=1e-5)

NEG = -60.0


class Ctr:
    def __init__(self, sem):
        self.sem = sem
        self.n = 0

    def inc(self, inst, k=1):
        inst.then_inc(self.sem, k)
        self.n += k
        return self.n

    def dma(self, inst):
        return self.inc(inst, 16)


def host_consts(cfg):
    j = np.arange(128)[:, None]
    i = np.arange(128)[None, :]
    mask_c = np.where(j <= i, 0.0, NEG).astype(np.float32)
    mask_e = np.where(j > i, 0.0, NEG).astype(np.float32)
    ones_bf = np.ones((128, 1), dtype=ml_dtypes.bfloat16)
    ones_f32r = np.ones((1, 128), dtype=np.float32)
    mask_cb = (j <= i).astype(ml_dtypes.bfloat16)
    mask_eb = (j > i).astype(ml_dtypes.bfloat16)
    return {"mask_c": mask_c, "mask_e": mask_e, "ones_bf": ones_bf,
            "ones_f32r": ones_f32r, "mask_cb": mask_cb, "mask_eb": mask_eb}


def build(cfg):
    B, S, D, H, DH = cfg["B"], cfg["S"], cfg["D"], cfg["H"], cfg["DH"]
    WIN, CAP, ISH, IR = cfg["WIN"], cfg["CAP"], cfg["ISH"], cfg["IR"]
    NEXP, EPS = cfg["NEXP"], cfg["EPS"]
    SCALE = 1.0 / math.sqrt(DH)
    HT, ST, WT = D // 128, S // 128, WIN // 128
    BS, BST = B * S, B * ST
    IT, IRT = ISH // 128, IR // 128
    SUP = min(256, ISH)
    TC = min(1024, BS)
    NTC = BS // TC
    OH = D // 2
    CW = min(512, S)
    CPT = CW // 128
    NCH = S // CW
    DCK = (D + 511) // 512
    OCK = (OH + 511) // 512
    TCK = TC // 512 if TC >= 512 else 1
    assert D == H * DH and DH == 128

    def NIT(jt):
        return min(jt + WT, ST - 1) - jt + 1

    SLOTW = (WT + 1) * 128

    mk_at = {}
    run = 0
    for b in range(B):
        for hd in range(H):
            for jt in range(ST):
                run += 1 + (1 if NIT(jt) == WT + 1 else 0)
                mk_at[(b, hd, jt)] = run
    gps = (SUP // 128) * NTC
    nsup_sh, nsup_r = ISH // SUP, IR // SUP
    gu_done = [2 * gps * (i + 1) for i in range(nsup_sh + nsup_r)]

    nc = bass.Bass("TRN2", target_bir_lowering=False)

    x_d = nc.dram_tensor("x", [BS, D], F32, kind="ExternalInput").ap()
    wq_d = nc.dram_tensor("wq", [D, D], F32, kind="ExternalInput").ap()
    wk_d = nc.dram_tensor("wk", [D, D], F32, kind="ExternalInput").ap()
    wv_d = nc.dram_tensor("wv", [D, D], F32, kind="ExternalInput").ap()
    wo_d = nc.dram_tensor("wo", [D, D], F32, kind="ExternalInput").ap()
    r_d = nc.dram_tensor("router", [NEXP, D], F32, kind="ExternalInput").ap()
    gsh_d = nc.dram_tensor("gsh", [ISH, D], F32, kind="ExternalInput").ap()
    ush_d = nc.dram_tensor("ush", [ISH, D], F32, kind="ExternalInput").ap()
    dsh_d = nc.dram_tensor("dsh", [D, ISH], F32, kind="ExternalInput").ap()
    gr_d = nc.dram_tensor("gr", [IR, D], F32, kind="ExternalInput").ap()
    ur_d = nc.dram_tensor("ur", [IR, D], F32, kind="ExternalInput").ap()
    dr_d = nc.dram_tensor("dr", [D, IR], F32, kind="ExternalInput").ap()
    mask_c_d = nc.dram_tensor("mask_c", [128, 128], F32, kind="ExternalInput").ap()
    mask_e_d = nc.dram_tensor("mask_e", [128, 128], F32, kind="ExternalInput").ap()
    mask_cb_d = nc.dram_tensor("mask_cb", [128, 128], BF16, kind="ExternalInput").ap()
    mask_eb_d = nc.dram_tensor("mask_eb", [128, 128], BF16, kind="ExternalInput").ap()
    ones_bf_d = nc.dram_tensor("ones_bf", [128, 1], BF16, kind="ExternalInput").ap()
    ones_f32r_d = nc.dram_tensor("ones_f32r", [1, 128], F32, kind="ExternalInput").ap()
    out_d = nc.dram_tensor("out", [BS, D], F32, kind="ExternalOutput").ap()

    def idram(name, shape, dt=BF16):
        return nc.dram_tensor(name, shape, dt, kind="Internal").ap()

    wq_bf = idram("wq_bf", [D, D]); wk_bf = idram("wk_bf", [D, D])
    wv_bf = idram("wv_bf", [D, D]); wo_bf = idram("wo_bf", [D, D])
    r_bf = idram("r_bf", [16, D])
    gsh_bf = idram("gsh_bf", [ISH, D]); ush_bf = idram("ush_bf", [ISH, D])
    dsh_bf = idram("dsh_bf", [D, ISH])
    gr_bf = idram("gr_bf", [IR, D]); ur_bf = idram("ur_bf", [IR, D])
    dr_bf = idram("dr_bf", [D, IR])
    xn_dram = idram("xn_dram", [BS, D])
    xnT_dram = idram("xnT_dram", [D, BS])
    q_tok = idram("q_tok", [BS, D]); k_tok = idram("k_tok", [BS, D])
    v_tok = idram("v_tok", [BS, D])
    resid1 = idram("resid1", [BS, D], F32)
    h_dram = idram("h_dram", [ISH, BS])
    hr_dram = idram("hr_dram", [IR, BS])

    C = {}
    g_es = ExitStack()

    def mkctr(name):
        return Ctr(g_es.enter_context(nc.semaphore(name)))

    for nm in ["h1", "h2", "bn", "sd", "xn", "xtl", "wt", "qmm", "qev",
               "cst", "qk", "th", "mk", "ex", "pv", "dn", "rc", "bc", "rr", "pve",
               "wol", "omm", "oev", "rt", "lmm", "lcb", "e2c", "gumm", "sil",
               "hmu", "xn2", "dtl", "dmm", "dev"]:
        C[nm] = mkctr(nm)
    M = {}
    for nm, k in [("xl", 3), ("xns", 2), ("xts", 2), ("qst", 2), ("hl", 2),
                  ("xc", 3), ("oxl", 3), ("ost", 2), ("r1l", 2), ("htl", 2),
                  ("gul", 2), ("hst", 2), ("dst", 2)]:
        M[nm] = [mkctr(f"{nm}_{i}") for i in range(k)]

    mask_c_sb = g_es.enter_context(nc.sbuf_tensor("mask_c_sb", [128, 128], F32))
    mask_e_sb = g_es.enter_context(nc.sbuf_tensor("mask_e_sb", [128, 128], F32))
    mask_cb_sb = g_es.enter_context(nc.sbuf_tensor("mask_cb_sb", [128, 128], BF16))
    mask_eb_sb = g_es.enter_context(nc.sbuf_tensor("mask_eb_sb", [128, 128], BF16))
    ones_bf_sb = g_es.enter_context(nc.sbuf_tensor("ones_bf_sb", [128, 1], BF16))
    ones_f32r_sb = g_es.enter_context(nc.sbuf_tensor("ones_f32r_sb", [1, 128], F32))
    cst_sb = g_es.enter_context(nc.sbuf_tensor("cst_sb", [128, 2], F32))
    onesrow_bf = g_es.enter_context(nc.sbuf_tensor("onesrow_bf", [1, 128], BF16))
    zrow_bf = g_es.enter_context(nc.sbuf_tensor("zrow_bf", [1, 512], BF16))
    comb_sb = g_es.enter_context(nc.sbuf_tensor("comb_sb", [128, BST], F32))
    nc.const_aps.register(F32, -float(CAP), cst_sb[:, 0:1])
    nc.const_aps.register(F32, float(EPS), cst_sb[:, 1:2])

    blk_es = ExitStack()
    block = blk_es.enter_context(nc.Block())

    @block.sync
    def _(sync):
        C["cst"].dma(sync.dma_start(mask_c_sb[:], mask_c_d[:]))
        C["cst"].dma(sync.dma_start(mask_e_sb[:], mask_e_d[:]))
        C["cst"].dma(sync.dma_start(mask_cb_sb[:], mask_cb_d[:]))
        C["cst"].dma(sync.dma_start(mask_eb_sb[:], mask_eb_d[:]))
        C["cst"].dma(sync.dma_start(ones_bf_sb[:], ones_bf_d[:]))
        C["cst"].dma(sync.dma_start(ones_f32r_sb[:], ones_f32r_d[:]))

    @block.vector
    def _(v):
        v.memset(cst_sb[:, 0:1], -float(CAP))
        v.memset(cst_sb[:, 1:2], float(EPS))
        v.memset(onesrow_bf[:], 1.0)
        C["cst"].inc(v.memset(zrow_bf[:], 0.0), 1)
    CST_ALL = C["cst"].n

    hop1 = [("wv", wv_bf[:], wv_d[:]), ("wk", wk_bf[:], wk_d[:]),
            ("wq", wq_bf[:], wq_d[:]), ("wo", wo_bf[:], wo_d[:]),
            ("r", r_bf[0:NEXP, :], r_d[:]),
            ("r2", r_bf[NEXP:2 * NEXP, :], r_d[:]),
            ("gsh", gsh_bf[:], gsh_d[:]), ("ush", ush_bf[:], ush_d[:]),
            ("gr", gr_bf[:], gr_d[:]), ("ur", ur_bf[:], ur_d[:]),
            ("dsh", dsh_bf[:], dsh_d[:]), ("dr", dr_bf[:], dr_d[:])]
    H1 = {}
    nn = 0
    for nm, _dst, _src in hop1:
        nn += 16
        H1[nm] = nn
    H1["r"] = H1["r2"]
    XTS_ALL_PRE = (16 * ((HT + 1) // 2), 16 * (HT // 2))
    _nxts = B * HT
    XTS_B_PRE = (16 * ((_nxts + 1) // 2), 16 * (_nxts // 2))

    @block.gpsimd
    def _(g):
        for i, (nm, dst, srcp) in enumerate(hop1):
            if i > 0:
                g.wait_ge(C["h1"].sem, C["h1"].n)
            if nm == "wv":
                g.wait_ge(M["xts"][0].sem, XTS_ALL_PRE[0])
                g.wait_ge(M["xts"][1].sem, XTS_ALL_PRE[1])
            if nm == "gsh":
                g.wait_ge(M["xts"][0].sem, XTS_B_PRE[0])
                g.wait_ge(M["xts"][1].sem, XTS_B_PRE[1])
            C["h1"].dma(g.dma_start(dst, srcp))

    p0_es = ExitStack()
    x_sb = [p0_es.enter_context(nc.sbuf_tensor(f"x_sb{i}", [128, D], F32))
            for i in range(3)]
    xn_st = [p0_es.enter_context(nc.sbuf_tensor(f"xn_st{i}", [128, D], BF16))
             for i in range(2)]
    st6 = [p0_es.enter_context(nc.sbuf_tensor(f"st6_{i}", [128, 6 * DCK], F32))
           for i in range(2)]
    st2 = [p0_es.enter_context(nc.sbuf_tensor(f"st2_{i}", [128, 2], F32))
           for i in range(2)]
    sdv = [p0_es.enter_context(nc.sbuf_tensor(f"sdv{i}", [128, 1], F32))
           for i in range(2)]
    rstd = [p0_es.enter_context(nc.sbuf_tensor(f"rstd{i}", [128, 1], F32))
            for i in range(2)]
    xt_st = [p0_es.enter_context(nc.sbuf_tensor(f"xt_st{i}", [128, S], BF16))
             for i in range(2)]

    for b in range(B):
        @block.sync
        def _(sync, b=b):
            for st in range(ST):
                gi = b * ST + st
                if gi >= 3:
                    sync.wait_ge(C["xn"].sem, gi - 2)
                M["xl"][gi % 3].dma(sync.dma_start(
                    x_sb[gi % 3][:], x_d[gi * 128:(gi + 1) * 128, :]))
                if st >= 1:
                    sync.wait_ge(C["xn"].sem, gi)
                    M["xns"][(gi - 1) % 2].dma(sync.dma_start(
                        xn_dram[(gi - 1) * 128:gi * 128, :],
                        xn_st[(gi - 1) % 2][:]))
            sync.wait_ge(C["xn"].sem, b * ST + ST)
            gi = b * ST + ST - 1
            M["xns"][gi % 2].dma(sync.dma_start(
                xn_dram[gi * 128:(gi + 1) * 128, :], xn_st[gi % 2][:]))

        @block.vector
        def _(v, b=b):
            for st in range(ST):
                p = st % 2
                gi = b * ST + st
                v.wait_ge(M["xl"][gi % 3].sem, 16 * (gi // 3 + 1))
                if gi >= 2:
                    v.wait_ge(C["bn"].sem, 3 * (gi - 1))
                    v.wait_ge(M["xns"][gi % 2].sem, 16 * (gi // 2))
                for ck in range(DCK):
                    w = min(512, D - ck * 512)
                    vi = v.bn_stats(st6[p][:, ck * 6:(ck + 1) * 6],
                                    x_sb[gi % 3][:, ck * 512:ck * 512 + w])
                C["bn"].inc(vi, 1)
                v.wait_ge(C["bn"].sem, C["bn"].n)
                C["bn"].inc(v.bn_aggr(st2[p][:], st6[p][:]), 1)
                v.wait_ge(C["sd"].sem, gi + 1)
                C["bn"].inc(v.reciprocal(rstd[p][:], sdv[p][:]), 1)
                v.wait_ge(C["bn"].sem, C["bn"].n)
                C["xn"].inc(v.tensor_scalar(
                    xn_st[p][:], x_sb[gi % 3][:], st2[p][:, 0:1], rstd[p][:],
                    ALU.subtract, ALU.mult), 1)

        @block.scalar
        def _(s, b=b):
            if b == 0:
                s.wait_ge(C["cst"].sem, CST_ALL)
            for st in range(ST):
                p = st % 2
                gi = b * ST + st
                s.wait_ge(C["bn"].sem, 3 * gi + 2)
                C["sd"].inc(s.activation(out=sdv[p][:], in_=st2[p][:, 1:2],
                                         func=AF.Sqrt, bias=float(EPS)), 1)

        @block.sync
        def _(sync, b=b):
            sync.wait_ge(M["xns"][0].sem, 16 * ((ST * (b + 1) + 1) // 2))
            sync.wait_ge(M["xns"][1].sem, 16 * (ST * (b + 1) // 2))
            for ht in range(HT):
                p = ht % 2
                gi = b * HT + ht
                if gi >= 2:
                    sync.wait_ge(M["xts"][p].sem, 16 * (gi // 2))
                C["xtl"].dma(sync.dma_start(
                    xt_st[p][:], xn_dram[b * S:(b + 1) * S,
                                         ht * 128:(ht + 1) * 128],
                    transpose=True))
                sync.wait_ge(C["xtl"].sem, C["xtl"].n)
                M["xts"][p].dma(sync.dma_start(
                    xnT_dram[ht * 128:(ht + 1) * 128, b * S:(b + 1) * S],
                    xt_st[p][:]))
        if b == 0:
            XTS_B0 = (M["xts"][0].n, M["xts"][1].n)
    XTS_ALL = (M["xts"][0].n, M["xts"][1].n)

    def wait_xts(eng):
        eng.wait_ge(M["xts"][0].sem, XTS_ALL[0])
        eng.wait_ge(M["xts"][1].sem, XTS_ALL[1])

    p1a_es = ExitStack()
    wt_buf = [p1a_es.enter_context(nc.sbuf_tensor(f"wt_buf{i}", [128, HT * D], BF16))
              for i in range(2)]
    xc_buf = [p1a_es.enter_context(nc.sbuf_tensor(f"xc_buf{i}", [128, HT * 128], BF16))
              for i in range(3)]
    qev = [p1a_es.enter_context(nc.sbuf_tensor(f"qev{i}", [128, D], BF16))
           for i in range(2)]
    p1a_ps = ExitStack()
    ps_q = [p1a_ps.enter_context(nc.psum_tensor(f"ps_q{i}", [128, D], F32))
            for i in range(2)]

    p1a_W = [("wv", wv_bf, v_tok), ("wk", wk_bf, k_tok), ("wq", wq_bf, q_tok)]
    NW = len(p1a_W)
    QST_B0 = []

    @block.sync
    def _(sync):
        for wi, (nm, wbf, dstd) in enumerate(p1a_W):
            wp = wi % 2
            sync.wait_ge(C["h1"].sem, H1[nm])
            if wi >= 2:
                sync.wait_ge(C["qmm"].sem, (wi - 1) * BST)
            for ht in range(HT):
                C["wt"].dma(sync.dma_start(
                    wt_buf[wp][:, ht * D:(ht + 1) * D],
                    wbf[0:D, ht * 128:(ht + 1) * 128], transpose=True))
            for st in range(BST):
                gi = wi * BST + st
                if st == 0 and wi == 0:
                    sync.wait_ge(M["xts"][0].sem, XTS_B0[0])
                    sync.wait_ge(M["xts"][1].sem, XTS_B0[1])
                if st == ST and wi == 0:
                    wait_xts(sync)
                if gi >= 3:
                    sync.wait_ge(C["qmm"].sem, gi - 2)
                M["xc"][gi % 3].dma(sync.dma_start(
                    xc_buf[gi % 3].ap().rearrange("p (ht s) -> p ht s", ht=HT),
                    xnT_dram[:, st * 128:(st + 1) * 128]
                    .rearrange("(ht p) s -> p ht s", p=128)))
                if gi >= 1:
                    sync.wait_ge(C["qev"].sem, gi)
                    pst, pw = divmod(gi - 1, BST)
                    M["qst"][(gi - 1) % 2].dma(sync.dma_start(
                        p1a_W[pst][2][pw * 128:(pw + 1) * 128, :],
                        qev[(gi - 1) % 2][:]))
                    if pst == 2 and pw == ST - 1:
                        QST_B0.append((M["qst"][0].n, M["qst"][1].n))
        sync.wait_ge(C["qev"].sem, NW * BST)
        M["qst"][(NW * BST - 1) % 2].dma(sync.dma_start(
            q_tok[(BST - 1) * 128:BST * 128, :], qev[(NW * BST - 1) % 2][:]))
        if not QST_B0:
            QST_B0.append((M["qst"][0].n, M["qst"][1].n))
    QST_ALL = (M["qst"][0].n, M["qst"][1].n)

    @block.tensor
    def _(t):
        for wi, (nm, wbf, dstd) in enumerate(p1a_W):
            wp = wi % 2
            t.wait_ge(C["wt"].sem, 16 * HT * (wi + 1))
            for st in range(BST):
                gi = wi * BST + st
                p = gi % 2
                t.wait_ge(M["xc"][gi % 3].sem, 16 * (gi // 3 + 1))
                if gi >= 2:
                    t.wait_ge(C["qev"].sem, gi - 1)
                for ht in range(HT):
                    for ck in range(DCK):
                        w = min(512, D - ck * 512)
                        mi = t.matmul(
                            ps_q[p][:, ck * 512:ck * 512 + w],
                            xc_buf[gi % 3][:, ht * 128:(ht + 1) * 128],
                            wt_buf[wp][:, ht * D + ck * 512:
                                        ht * D + ck * 512 + w],
                            start=(ht == 0), stop=(ht == HT - 1))
                C["qmm"].inc(mi, 1)

    @block.vector
    def _(v):
        for gi in range(NW * BST):
            p = gi % 2
            v.wait_ge(C["qmm"].sem, gi + 1)
            if gi >= 2:
                v.wait_ge(M["qst"][p].sem, 16 * (gi // 2))
            C["qev"].inc(v.tensor_copy(qev[p][:], ps_q[p][:]), 1)

    QMM_ALL = NW * BST
    p1a_ps.close()
    p1a_es.close()
    p0_es.close()

    def chunk_jts(ci):
        c0, c1 = ci * CPT, ci * CPT + CPT - 1
        res = []
        for jt in range(max(0, c0 - WT), min(ST - 1, c1) + 1):
            lo, hi = max(c0, jt), min(c1, jt + WT, ST - 1)
            if lo <= hi:
                res.append((jt, lo, hi))
        return res

    for b in range(B):
        atk_es = ExitStack()
        attnT_sb = atk_es.enter_context(
            nc.sbuf_tensor(f"attnT{b}", [128, H * S], BF16))
        att_es = ExitStack()
        qt_sb = [att_es.enter_context(nc.sbuf_tensor(f"qt{b}_{i}", [128, S], BF16))
                 for i in range(2)]
        kt_sb = [att_es.enter_context(nc.sbuf_tensor(f"kt{b}_{i}", [128, S], BF16))
                 for i in range(2)]
        v_sb = [att_es.enter_context(nc.sbuf_tensor(f"v{b}_{i}", [128, ST * DH], BF16))
                for i in range(2)]
        pt_sb = [att_es.enter_context(
            nc.sbuf_tensor(f"pt{b}_{i}", [128, ST * SLOTW], BF16)) for i in range(2)]
        t_sb = [att_es.enter_context(nc.sbuf_tensor(f"t{b}_{i}", [128, SLOTW], F32))
                for i in range(2)]
        rec_sb = att_es.enter_context(nc.sbuf_tensor(f"rec{b}", [1, S], F32))
        rrep_sb = [att_es.enter_context(nc.sbuf_tensor(f"rrep{b}_{i}", [128, CW], F32))
                   for i in range(4)]
        att_ps = ExitStack()
        ps_sT = [att_ps.enter_context(nc.psum_tensor(f"ps_sT{b}_{i}", [128, 1536], F32))
                 for i in range(2)]
        ps_pv = att_ps.enter_context(nc.psum_tensor(f"ps_pv{b}", [128, 512], F32))
        ps_db = att_ps.enter_context(nc.psum_tensor(f"ps_db{b}", [128, 512], F32))

        @block.sync
        def _(sync, b=b):
            if b == 0:
                sync.wait_ge(M["qst"][0].sem, QST_ALL[0])
                sync.wait_ge(M["qst"][1].sem, QST_ALL[1])
                sync.wait_ge(C["qev"].sem, QMM_ALL)
            else:
                sync.wait_ge(M["ost"][0].sem, 16 * ((ST * b + 1) // 2))
                sync.wait_ge(M["ost"][1].sem, 16 * (ST * b // 2))
                sync.wait_ge(C["oev"].sem, ST * b)
                sync.wait_ge(C["omm"].sem, ST * b)
            for hd in range(H):
                p = hd % 2
                gi = b * H + hd
                if gi >= 2:
                    sync.wait_ge(C["pve"].sem, (gi - 1) * NCH)
                for src, dstb in ((q_tok, qt_sb), (k_tok, kt_sb)):
                    M["hl"][p].dma(sync.dma_start(
                        dstb[p][:],
                        src[b * S:(b + 1) * S, hd * DH:(hd + 1) * DH],
                        transpose=True))
                M["hl"][p].dma(sync.dma_start(
                    v_sb[p].ap().rearrange("p (jt d) -> p jt d", jt=ST),
                    v_tok[b * S:(b + 1) * S, hd * DH:(hd + 1) * DH]
                    .rearrange("(jt p) d -> p jt d", p=128)))

        @block.tensor
        def _(t, b=b):
            if b == 0:
                t.wait_ge(C["cst"].sem, CST_ALL)
                t.wait_ge(C["qev"].sem, QMM_ALL)
            else:
                t.wait_ge(C["oev"].sem, ST * b)
                t.wait_ge(M["ost"][0].sem, 16 * ((ST * b + 1) // 2))
                t.wait_ge(M["ost"][1].sem, 16 * (ST * b // 2))
            def emit_qk_jt(hd, jt):
                p = hd % 2
                gi = b * H + hd
                nit = NIT(jt)
                wlen = nit * 128
                tj = gi * ST + jt
                if tj >= 2:
                    t.wait_ge(C["th"].sem, tj - 1)
                for ck in range((wlen + 511) // 512):
                    w = min(512, wlen - ck * 512)
                    mi = t.matmul(
                        ps_sT[tj % 2][:, ck * 512:ck * 512 + w],
                        kt_sb[p][:, jt * 128:(jt + 1) * 128],
                        qt_sb[p][:, jt * 128 + ck * 512:
                                  jt * 128 + ck * 512 + w],
                        start=True, stop=True)
                C["qk"].inc(mi, 1)

            def emit_den(hd, ci):
                p = hd % 2
                gi = b * H + hd
                jts = chunk_jts(ci)
                k = gi * NCH + ci
                t.wait_ge(C["mk"].sem,
                          mk_at[(b, hd, max(j for j, _, _ in jts))])
                if k >= 1:
                    t.wait_ge(C["rr"].sem, k)
                t.matmul(ps_db[0:1, 0:CW], onesrow_bf[0:1, 0:1],
                         zrow_bf[0:1, 0:CW], start=True, stop=False)
                for idx, (jt, lo, hi) in enumerate(jts):
                    w = (hi - lo + 1) * 128
                    mi2 = t.matmul(
                        ps_db[0:1, (lo - ci * CPT) * 128:
                                   (lo - ci * CPT) * 128 + w],
                        ones_bf_sb[:],
                        pt_sb[p][:, jt * SLOTW + (lo - jt) * 128:
                                  jt * SLOTW + (lo - jt) * 128 + w],
                        start=False, stop=(idx == len(jts) - 1))
                C["dn"].inc(mi2, 1)

            def emit_bc(hd, ci):
                gi = b * H + hd
                k = gi * NCH + ci
                t.wait_ge(C["rc"].sem, k + 1)
                C["bc"].inc(t.matmul(
                    ps_db[:, 0:CW], ones_f32r_sb[:],
                    rec_sb[0:1, ci * CW:(ci + 1) * CW],
                    start=True, stop=True), 1)

            def emit_pv(hd, ci):
                p = hd % 2
                gi = b * H + hd
                jts = chunk_jts(ci)
                k = gi * NCH + ci
                if k >= 1:
                    t.wait_ge(C["pve"].sem, k)
                t.matmul(ps_pv[:, 0:CW], onesrow_bf[:],
                         zrow_bf[0:1, 0:CW], start=True, stop=False)
                mi = None
                for idx, (jt, lo, hi) in enumerate(jts):
                    w = (hi - lo + 1) * 128
                    mi = t.matmul(
                        ps_pv[:, (lo - ci * CPT) * 128:
                              (lo - ci * CPT) * 128 + w],
                        v_sb[p][:, jt * DH:(jt + 1) * DH],
                        pt_sb[p][:, jt * SLOTW + (lo - jt) * 128:
                                  jt * SLOTW + (lo - jt) * 128 + w],
                        start=False, stop=(idx == len(jts) - 1))
                C["pv"].inc(mi, 1)

            for hd in range(H):
                p = hd % 2
                gi = b * H + hd
                t.wait_ge(M["hl"][p].sem, 48 * (gi // 2 + 1))
                for jt in range(ST):
                    emit_qk_jt(hd, jt)
                    if (jt + 1) % CPT == 0:
                        ci = (jt + 1) // CPT - 1
                        if hd >= 1:
                            emit_pv(hd - 1, ci)
                        if ci >= 1:
                            emit_bc(hd, ci - 1)
                        emit_den(hd, ci)
                emit_bc(hd, NCH - 1)
            for ci in range(NCH):
                emit_pv(H - 1, ci)

        @block.scalar
        def _(s, b=b):
            if b:
                s.wait_ge(M["ost"][0].sem, 16 * ((ST * b + 1) // 2))
                s.wait_ge(M["ost"][1].sem, 16 * (ST * b // 2))
            for hd in range(H):
                p = hd % 2
                gi = b * H + hd
                if gi >= 2:
                    s.wait_ge(C["pve"].sem, (gi - 1) * NCH)
                for jt in range(ST):
                    nit = NIT(jt)
                    tj = gi * ST + jt
                    s.wait_ge(C["qk"].sem, tj + 1)
                    if tj >= 2:
                        s.wait_ge(C["ex"].sem, tj - 1)
                    C["th"].inc(s.activation(
                        out=t_sb[tj % 2][:, 0:nit * 128],
                        in_=ps_sT[tj % 2][:, 0:nit * 128], func=AF.Tanh,
                        scale=float(SCALE / CAP)), 1)
                    s.wait_ge(C["th"].sem, C["th"].n)
                    C["ex"].inc(s.activation(
                        out=pt_sb[p][:, jt * SLOTW:jt * SLOTW + nit * 128],
                        in_=t_sb[tj % 2][:, 0:nit * 128], func=AF.Exp,
                        scale=float(CAP), bias=-float(CAP)), 1)

        @block.gpsimd
        def _(g, b=b):
            for hd in range(H):
                p = hd % 2
                gi = b * H + hd
                for jt in range(ST):
                    nit = NIT(jt)
                    tj = gi * ST + jt
                    g.wait_ge(C["ex"].sem, tj + 1)
                    sl = pt_sb[p][:, jt * SLOTW:jt * SLOTW + 128]
                    mi = g.tensor_mul(sl, sl, mask_cb_sb[:])
                    C["mk"].inc(mi, 1)
                    if nit == WT + 1:
                        sl = pt_sb[p][:, jt * SLOTW + WT * 128:
                                      jt * SLOTW + (WT + 1) * 128]
                        mi = g.tensor_mul(sl, sl, mask_eb_sb[:])
                        C["mk"].inc(mi, 1)
                    assert C["mk"].n == mk_at[(b, hd, jt)]

        @block.vector
        def _(v, b=b):
            for hd in range(H):
                gi = b * H + hd
                for ci in range(NCH):
                    k = gi * NCH + ci
                    if hd >= 1:
                        kp = (gi - 1) * NCH + ci
                        v.wait_ge(C["pv"].sem, kp + 1)
                        v.wait_ge(C["rr"].sem, kp + 1)
                        C["pve"].inc(v.tensor_tensor(
                            attnT_sb[:, (hd - 1) * S + ci * CW:
                                     (hd - 1) * S + (ci + 1) * CW],
                            ps_pv[:, 0:CW], rrep_sb[kp % 4][:, 0:CW],
                            ALU.mult), 1)
                    v.wait_ge(C["dn"].sem, k + 1)
                    if k >= NCH:
                        v.wait_ge(C["bc"].sem, k - NCH + 1)
                    C["rc"].inc(v.reciprocal(
                        rec_sb[0:1, ci * CW:(ci + 1) * CW],
                        ps_db[0:1, 0:CW]), 1)
                    v.wait_ge(C["bc"].sem, k + 1)
                    if k >= 4:
                        v.wait_ge(C["pve"].sem, k - 3)
                    C["rr"].inc(v.tensor_copy(rrep_sb[k % 4][:, 0:CW],
                                              ps_db[:, 0:CW]), 1)
            for ci in range(NCH):
                kp = (b * H + H - 1) * NCH + ci
                v.wait_ge(C["pv"].sem, kp + 1)
                v.wait_ge(C["rr"].sem, kp + 1)
                C["pve"].inc(v.tensor_tensor(
                    attnT_sb[:, (H - 1) * S + ci * CW:
                             (H - 1) * S + (ci + 1) * CW],
                    ps_pv[:, 0:CW], rrep_sb[kp % 4][:, 0:CW],
                    ALU.mult), 1)

        att_ps.close()
        att_es.close()
        oc_es = ExitStack()
        wo_sb = oc_es.enter_context(nc.sbuf_tensor(f"wo_sb{b}", [128, HT * D], BF16))
        ox_sb = [oc_es.enter_context(nc.sbuf_tensor(f"ox{b}_{i}", [128, D], F32))
                 for i in range(3)]
        oev = [oc_es.enter_context(nc.sbuf_tensor(f"oev{b}_{i}", [128, D], F32))
               for i in range(2)]
        oc_ps = ExitStack()
        ps_o = [oc_ps.enter_context(nc.psum_tensor(f"ps_o{b}_{i}", [128, D], F32))
                for i in range(2)]

        @block.sync
        def _(sync, b=b):
            sync.wait_ge(C["h1"].sem, H1["wo"])
            sync.wait_ge(C["pve"].sem, (b + 1) * H * NCH)
            for ht in range(HT):
                C["wol"].dma(sync.dma_start(
                    wo_sb[:, ht * D:(ht + 1) * D],
                    wo_bf[0:D, ht * 128:(ht + 1) * 128], transpose=True))

        @block.scalar
        def _(s, b=b):
            s.wait_ge(C["pve"].sem, (b + 1) * H * NCH)
            for st in range(ST):
                gi = b * ST + st
                p3 = gi % 3
                if st >= 3:
                    s.wait_ge(C["oev"].sem, gi - 2)
                M["oxl"][p3].dma(s.dma_start(
                    ox_sb[p3][:], x_d[gi * 128:(gi + 1) * 128, :]))
                if st >= 2:
                    s.wait_ge(C["oev"].sem, gi - 1)
                    M["ost"][(gi - 2) % 2].dma(s.dma_start(
                        resid1[(gi - 2) * 128:(gi - 1) * 128, :],
                        oev[(gi - 2) % 2][:]))
            s.wait_ge(C["oev"].sem, (b + 1) * ST)
            for st in (ST - 2, ST - 1):
                gi = b * ST + st
                M["ost"][gi % 2].dma(s.dma_start(
                    resid1[gi * 128:(gi + 1) * 128, :], oev[gi % 2][:]))

        @block.tensor
        def _(t, b=b):
            t.wait_ge(C["wol"].sem, 16 * HT * (b + 1))
            t.wait_ge(C["pve"].sem, (b + 1) * H * NCH)
            for st in range(ST):
                p = st % 2
                gi = b * ST + st
                if st >= 2:
                    t.wait_ge(C["oev"].sem, gi - 1)
                for ht in range(HT):
                    for ck in range(DCK):
                        w = min(512, D - ck * 512)
                        mi = t.matmul(
                            ps_o[p][:, ck * 512:ck * 512 + w],
                            attnT_sb[:, ht * S + st * 128:ht * S + (st + 1) * 128],
                            wo_sb[:, ht * D + ck * 512:ht * D + ck * 512 + w],
                            start=(ht == 0), stop=(ht == HT - 1))
                C["omm"].inc(mi, 1)

        @block.vector
        def _(v, b=b):
            for st in range(ST):
                p = st % 2
                gi = b * ST + st
                v.wait_ge(C["omm"].sem, gi + 1)
                v.wait_ge(M["oxl"][gi % 3].sem, 16 * (gi // 3 + 1))
                if st >= 2:
                    v.wait_ge(M["ost"][p].sem, 16 * (gi // 2))
                C["oev"].inc(v.tensor_add(oev[p][:], ps_o[p][:], ox_sb[gi % 3][:]), 1)

        oc_ps.close()
        oc_es.close()
        atk_es.close()

    OST_ALL = (M["ost"][0].n, M["ost"][1].n)
    OEV_ALL = C["oev"].n

    def wait_ost(eng):
        eng.wait_ge(M["ost"][0].sem, OST_ALL[0])
        eng.wait_ge(M["ost"][1].sem, OST_ALL[1])

    p2a_es = ExitStack()
    xnT_sb = p2a_es.enter_context(nc.sbuf_tensor("xnT_sb", [128, HT * BS], BF16))
    rt_sb = p2a_es.enter_context(nc.sbuf_tensor("rt_sb", [128, HT * 16], BF16))
    lg_sb = [p2a_es.enter_context(nc.sbuf_tensor(f"lg{i}", [128, NEXP], F32))
             for i in range(2)]
    m1_sb = p2a_es.enter_context(nc.sbuf_tensor("m1_sb", [128, 1], F32))
    eq_sb = p2a_es.enter_context(nc.sbuf_tensor("eq_sb", [128, NEXP], F32))
    m2_sb = p2a_es.enter_context(nc.sbuf_tensor("m2_sb", [128, 1], F32))
    e2_sb = p2a_es.enter_context(nc.sbuf_tensor("e2_sb", [128, 1], F32))
    dn_sb = p2a_es.enter_context(nc.sbuf_tensor("dn_sb", [128, 1], F32))
    rc2_sb = p2a_es.enter_context(nc.sbuf_tensor("rc2_sb", [128, 1], F32))
    gt_buf = [p2a_es.enter_context(nc.sbuf_tensor(f"gt_buf{i}", [128, HT * SUP], BF16))
              for i in range(2)]
    ut_buf = [p2a_es.enter_context(nc.sbuf_tensor(f"ut_buf{i}", [128, HT * SUP], BF16))
              for i in range(2)]
    sil_sb = [p2a_es.enter_context(nc.sbuf_tensor(f"sil{i}", [128, TC], F32))
              for i in range(2)]
    h_st = [p2a_es.enter_context(nc.sbuf_tensor(f"h_st{i}", [128, TC], BF16))
            for i in range(2)]
    p2a_ps1 = ExitStack()
    ps_l = [p2a_ps1.enter_context(nc.psum_tensor(f"ps_l{i}", [128, 512], F32))
            for i in range(2)]

    p2a_cfgs = [("sh", gsh_bf, ush_bf, h_dram, ISH),
                ("r", gr_bf, ur_bf, hr_dram, IR)]
    LCB_ALL = 10 * BST

    @block.sync
    def _(sync):
        wait_ost(sync)
        sync.wait_ge(C["oev"].sem, OEV_ALL)
        C["xn2"].dma(sync.dma_start(
            xnT_sb.ap().rearrange("p (ht s) -> p ht s", ht=HT),
            xnT_dram.rearrange("(ht p) s -> p ht s", p=128)))
        for ht in range(HT):
            C["rt"].dma(sync.dma_start(
                rt_sb[:, ht * 16:(ht + 1) * 16],
                r_bf[0:16, ht * 128:(ht + 1) * 128], transpose=True))
        for seg, (nm, gtc, utc, hdst, II) in enumerate(p2a_cfgs):
            sync.wait_ge(C["h1"].sem, H1["gsh" if nm == "sh" else "gr"])
            sync.wait_ge(C["h1"].sem, H1["ush" if nm == "sh" else "ur"])
            base = 0 if nm == "sh" else nsup_sh
            for sup in range(II // SUP):
                gsup = base + sup
                p = gsup % 2
                if gsup >= 2:
                    sync.wait_ge(C["gumm"].sem, gu_done[gsup - 2])
                for ht in range(HT):
                    M["gul"][p].dma(sync.dma_start(
                        gt_buf[p][:, ht * SUP:(ht + 1) * SUP],
                        gtc[sup * SUP:(sup + 1) * SUP,
                            ht * 128:(ht + 1) * 128], transpose=True))
                    M["gul"][p].dma(sync.dma_start(
                        ut_buf[p][:, ht * SUP:(ht + 1) * SUP],
                        utc[sup * SUP:(sup + 1) * SUP,
                            ht * 128:(ht + 1) * 128], transpose=True))

    @block.tensor
    def _(t):
        t.wait_ge(C["xn2"].sem, 16)
        t.wait_ge(C["rt"].sem, 16 * HT)
        t.wait_ge(C["oev"].sem, OEV_ALL)
        wait_ost(t)
        for st in range(BST):
            p = st % 2
            if st >= 2:
                t.wait_ge(C["lcb"].sem, 10 * (st - 1))
            for ht in range(HT):
                mi = t.matmul(ps_l[p][:, 0:NEXP],
                              xnT_sb[:, ht * BS + st * 128:
                                     ht * BS + (st + 1) * 128],
                              rt_sb[:, ht * 16:ht * 16 + NEXP],
                              start=(ht == 0), stop=(ht == HT - 1))
            C["lmm"].inc(mi, 1)

    @block.vector
    def _(v):
        wait_ost(v)
        for st in range(BST):
            p = st % 2
            v.wait_ge(C["lmm"].sem, st + 1)
            if st >= 1:
                v.wait_ge(C["e2c"].sem, st)
            C["lcb"].inc(v.tensor_copy(lg_sb[p][:], ps_l[p][:, 0:NEXP]), 1)
            v.wait_ge(C["lcb"].sem, C["lcb"].n)
            C["lcb"].inc(v.reduce_max(m1_sb[:], lg_sb[p][:], axis=AX), 1)
            v.wait_ge(C["lcb"].sem, C["lcb"].n)
            C["lcb"].inc(v.tensor_scalar(eq_sb[:], lg_sb[p][:], m1_sb[:], None,
                                         ALU.is_equal), 1)
            v.wait_ge(C["lcb"].sem, C["lcb"].n)
            C["lcb"].inc(v.scalar_tensor_tensor(eq_sb[:], eq_sb[:], -1e30,
                                                lg_sb[p][:], ALU.mult, ALU.add), 1)
            v.wait_ge(C["lcb"].sem, C["lcb"].n)
            C["lcb"].inc(v.reduce_max(m2_sb[:], eq_sb[:], axis=AX), 1)
            v.wait_ge(C["lcb"].sem, C["lcb"].n)
            C["lcb"].inc(v.tensor_sub(m2_sb[:], m2_sb[:], m1_sb[:]), 1)
            v.wait_ge(C["e2c"].sem, st + 1)
            C["lcb"].inc(v.tensor_scalar(dn_sb[:], e2_sb[:], 1.0, None, ALU.add), 1)
            v.wait_ge(C["lcb"].sem, C["lcb"].n)
            C["lcb"].inc(v.reciprocal(rc2_sb[:], dn_sb[:]), 1)
            v.wait_ge(C["lcb"].sem, C["lcb"].n)
            C["lcb"].inc(v.scalar_tensor_tensor(comb_sb[:, st:st + 1], e2_sb[:],
                                                1.0, rc2_sb[:], ALU.mult,
                                                ALU.mult), 1)
            v.wait_ge(C["lcb"].sem, C["lcb"].n)
            C["lcb"].inc(v.tensor_add(comb_sb[:, st:st + 1],
                                      comb_sb[:, st:st + 1], rc2_sb[:]), 1)

    @block.scalar
    def _(s):
        wait_ost(s)
        for st in range(BST):
            s.wait_ge(C["lcb"].sem, 6 + 10 * st)
            C["e2c"].inc(s.activation(out=e2_sb[:], in_=m2_sb[:], func=AF.Exp), 1)

    p2a_ps1.close()
    p2a_ps2 = ExitStack()
    ps_g = [p2a_ps2.enter_context(nc.psum_tensor(f"ps_g{i}", [128, TC], F32))
            for i in range(2)]
    ps_u = [p2a_ps2.enter_context(nc.psum_tensor(f"ps_u{i}", [128, TC], F32))
            for i in range(2)]

    @block.tensor
    def _(t):
        t.wait_ge(C["lcb"].sem, LCB_ALL)
        gr_i = 0
        for nm, gtc, utc, hdst, II in p2a_cfgs:
            base = 0 if nm == "sh" else nsup_sh
            for sup in range(II // SUP):
                gsup = base + sup
                p = gsup % 2
                t.wait_ge(M["gul"][p].sem, 16 * 2 * HT * (gsup // 2 + 1))
                for itl in range(SUP // 128):
                    for tc in range(NTC):
                        gp = gr_i % 2
                        if gr_i >= 2:
                            t.wait_ge(C["hmu"].sem, gr_i - 1)
                        for ht in range(HT):
                            for ck in range(TCK):
                                w = min(512, TC - ck * 512)
                                mi = t.matmul(
                                    ps_g[gp][:, ck * 512:ck * 512 + w],
                                    gt_buf[p][:, ht * SUP + itl * 128:
                                              ht * SUP + (itl + 1) * 128],
                                    xnT_sb[:, ht * BS + tc * TC + ck * 512:
                                           ht * BS + tc * TC + ck * 512 + w],
                                    start=(ht == 0), stop=(ht == HT - 1))
                        C["gumm"].inc(mi, 1)
                        for ht in range(HT):
                            for ck in range(TCK):
                                w = min(512, TC - ck * 512)
                                mi = t.matmul(
                                    ps_u[gp][:, ck * 512:ck * 512 + w],
                                    ut_buf[p][:, ht * SUP + itl * 128:
                                              ht * SUP + (itl + 1) * 128],
                                    xnT_sb[:, ht * BS + tc * TC + ck * 512:
                                           ht * BS + tc * TC + ck * 512 + w],
                                    start=(ht == 0), stop=(ht == HT - 1))
                        C["gumm"].inc(mi, 1)
                        gr_i += 1

    @block.scalar
    def _(s):
        gr_i = 0
        for nm, gtc, utc, hdst, II in p2a_cfgs:
            for sup in range(II // SUP):
                for itl in range(SUP // 128):
                    for tc in range(NTC):
                        gp = gr_i % 2
                        s.wait_ge(C["gumm"].sem, 2 * gr_i + 1)
                        if gr_i >= 2:
                            s.wait_ge(C["hmu"].sem, gr_i - 1)
                        C["sil"].inc(s.activation(out=sil_sb[gp][:],
                                                  in_=ps_g[gp][:],
                                                  func=AF.Silu), 1)
                        gr_i += 1

    @block.vector
    def _(v):
        gr_i = 0
        for nm, gtc, utc, hdst, II in p2a_cfgs:
            for sup in range(II // SUP):
                for itl in range(SUP // 128):
                    for tc in range(NTC):
                        gp = gr_i % 2
                        v.wait_ge(C["sil"].sem, gr_i + 1)
                        v.wait_ge(C["gumm"].sem, 2 * gr_i + 2)
                        if gr_i >= 2:
                            v.wait_ge(M["hst"][gp].sem, 16 * (gr_i // 2))
                        C["hmu"].inc(v.tensor_tensor(h_st[gp][:], sil_sb[gp][:],
                                                     ps_u[gp][:], ALU.mult), 1)
                        gr_i += 1

    @block.gpsimd
    def _(g):
        gr_i = 0
        for nm, gtc, utc, hdst, II in p2a_cfgs:
            for sup in range(II // SUP):
                for itl in range(SUP // 128):
                    ig = sup * (SUP // 128) + itl
                    for tc in range(NTC):
                        gp = gr_i % 2
                        g.wait_ge(C["hmu"].sem, gr_i + 1)
                        M["hst"][gp].dma(g.dma_start(
                            hdst[ig * 128:(ig + 1) * 128,
                                 tc * TC:(tc + 1) * TC], h_st[gp][:]))
                        gr_i += 1
    HST_ALL = (M["hst"][0].n, M["hst"][1].n)
    HMU_ALL = C["hmu"].n

    p2a_ps2.close()
    p2a_es.close()
    p2b_es = ExitStack()
    dt_sb = p2b_es.enter_context(nc.sbuf_tensor("dt_sb", [128, IT * OH], BF16))
    rdt_sb = p2b_es.enter_context(nc.sbuf_tensor("rdt_sb", [128, IRT * OH], BF16))
    htc = [p2b_es.enter_context(nc.sbuf_tensor(f"htc{i}", [128, IT * 128], BF16))
           for i in range(2)]
    hrc = [p2b_es.enter_context(nc.sbuf_tensor(f"hrc{i}", [128, IRT * 128], BF16))
           for i in range(2)]
    r1_sb = [p2b_es.enter_context(nc.sbuf_tensor(f"r1_{i}", [128, OH], F32))
             for i in range(2)]
    tmp_sb = [p2b_es.enter_context(nc.sbuf_tensor(f"tmp{i}", [128, OH], F32))
              for i in range(2)]
    o_st = [p2b_es.enter_context(nc.sbuf_tensor(f"o_st{i}", [128, OH], F32))
            for i in range(2)]
    p2b_ps = ExitStack()
    ps_sh = [p2b_ps.enter_context(nc.psum_tensor(f"ps_sh{i}", [128, OH], F32))
             for i in range(2)]
    ps_r = [p2b_ps.enter_context(nc.psum_tensor(f"ps_r{i}", [128, OH], F32))
            for i in range(2)]

    for oh in range(2):
        @block.sync
        def _(sync, oh=oh):
            if oh == 0:
                sync.wait_ge(C["h1"].sem, H1["dr"])
                sync.wait_ge(C["hmu"].sem, HMU_ALL)
                sync.wait_ge(M["hst"][0].sem, HST_ALL[0])
                sync.wait_ge(M["hst"][1].sem, HST_ALL[1])
            else:
                sync.wait_ge(C["dmm"].sem, 2 * BST)
            for it in range(IT):
                C["dtl"].dma(sync.dma_start(
                    dt_sb[:, it * OH:(it + 1) * OH],
                    dsh_bf[oh * OH:(oh + 1) * OH,
                           it * 128:(it + 1) * 128], transpose=True))
            for it in range(IRT):
                C["dtl"].dma(sync.dma_start(
                    rdt_sb[:, it * OH:(it + 1) * OH],
                    dr_bf[oh * OH:(oh + 1) * OH,
                          it * 128:(it + 1) * 128], transpose=True))
            for st in range(BST):
                gi = oh * BST + st
                p2 = gi % 2
                if gi >= 2:
                    sync.wait_ge(C["dmm"].sem, 2 * (gi - 1))
                M["htl"][p2].dma(sync.dma_start(
                    htc[p2].ap().rearrange("p (it s) -> p it s", it=IT),
                    h_dram[:, st * 128:(st + 1) * 128]
                    .rearrange("(it p) s -> p it s", p=128)))
                M["htl"][p2].dma(sync.dma_start(
                    hrc[p2].ap().rearrange("p (it s) -> p it s", it=IRT),
                    hr_dram[:, st * 128:(st + 1) * 128]
                    .rearrange("(it p) s -> p it s", p=128)))

        @block.scalar
        def _(s, oh=oh):
            if oh == 0:
                s.wait_ge(C["hmu"].sem, HMU_ALL)
                s.wait_ge(M["hst"][0].sem, HST_ALL[0])
                s.wait_ge(M["hst"][1].sem, HST_ALL[1])
            for st in range(BST):
                gi = oh * BST + st
                p2 = gi % 2
                if gi >= 2:
                    s.wait_ge(C["dev"].sem, 2 * (gi - 1))
                M["r1l"][p2].dma(s.dma_start(
                    r1_sb[p2][:], resid1[st * 128:(st + 1) * 128,
                                         oh * OH:(oh + 1) * OH]))
                if gi >= 2:
                    s.wait_ge(C["dev"].sem, 2 * (gi - 1))
                    pgi = gi - 2
                    pst, poh = pgi % BST, pgi // BST
                    M["dst"][pgi % 2].dma(s.dma_start(
                        out_d[pst * 128:(pst + 1) * 128,
                              poh * OH:(poh + 1) * OH], o_st[pgi % 2][:]))
            if oh == 1:
                s.wait_ge(C["dev"].sem, 4 * BST)
                for pgi in (2 * BST - 2, 2 * BST - 1):
                    pst, poh = pgi % BST, pgi // BST
                    M["dst"][pgi % 2].dma(s.dma_start(
                        out_d[pst * 128:(pst + 1) * 128,
                              poh * OH:(poh + 1) * OH], o_st[pgi % 2][:]))

        @block.tensor
        def _(t, oh=oh):
            if oh == 0:
                t.wait_ge(C["hmu"].sem, HMU_ALL)
            t.wait_ge(C["dtl"].sem, 16 * (IT + IRT) * (oh + 1))
            for st in range(BST):
                gi = oh * BST + st
                gp = gi % 2
                t.wait_ge(M["htl"][gp].sem, 32 * (gi // 2 + 1))
                if gi >= 2:
                    t.wait_ge(C["dev"].sem, 2 * (gi - 1))
                for it in range(IT):
                    for ck in range(OCK):
                        w = min(512, OH - ck * 512)
                        mi = t.matmul(
                            ps_sh[gp][:, ck * 512:ck * 512 + w],
                            htc[gp][:, it * 128:(it + 1) * 128],
                            dt_sb[:, it * OH + ck * 512:it * OH + ck * 512 + w],
                            start=(it == 0), stop=(it == IT - 1))
                C["dmm"].inc(mi, 1)
                for it in range(IRT):
                    for ck in range(OCK):
                        w = min(512, OH - ck * 512)
                        mi = t.matmul(
                            ps_r[gp][:, ck * 512:ck * 512 + w],
                            hrc[gp][:, it * 128:(it + 1) * 128],
                            rdt_sb[:, it * OH + ck * 512:it * OH + ck * 512 + w],
                            start=(it == 0), stop=(it == IRT - 1))
                C["dmm"].inc(mi, 1)

        @block.vector
        def _(v, oh=oh):
            for st in range(BST):
                gi = oh * BST + st
                gp = gi % 2
                v.wait_ge(C["dmm"].sem, 2 * (gi + 1))
                v.wait_ge(M["r1l"][gp].sem, 16 * (gi // 2 + 1))
                if gi >= 2:
                    v.wait_ge(M["dst"][gp].sem, 16 * (gi // 2))
                    v.wait_ge(C["dev"].sem, 2 * (gi - 1))
                C["dev"].inc(v.scalar_tensor_tensor(
                    tmp_sb[gp][:], ps_r[gp][:], comb_sb[:, st:st + 1],
                    r1_sb[gp][:], ALU.mult, ALU.add), 1)
                v.wait_ge(C["dev"].sem, C["dev"].n)
                C["dev"].inc(v.tensor_add(o_st[gp][:], tmp_sb[gp][:],
                                          ps_sh[gp][:]), 1)

    @block.sync
    def _(sync):
        sync.wait_ge(M["dst"][0].sem, M["dst"][0].n)
        sync.wait_ge(M["dst"][1].sem, M["dst"][1].n)

    p2b_ps.close()
    p2b_es.close()
    blk_es.close()
    g_es.close()
    return nc


def kernel(L0_attn_x, L0_attn_Wq, L0_attn_Wk, L0_attn_Wv, L0_attn_Wo,
           L0_ffn_router_weight, L0_ffn_sh_gate, L0_ffn_sh_up, L0_ffn_sh_down,
           L0_ffn_r_gate, L0_ffn_r_up, L0_ffn_r_down):
    cfg = FULL
    nc = build(cfg)
    cn = host_consts(cfg)
    x = np.asarray(L0_attn_x, dtype=np.float32)
    Bfull = x.shape[0]
    ncore = 8
    bpc = Bfull // ncore
    BS = bpc * cfg["S"]
    shared = {
        "wq": np.asarray(L0_attn_Wq, np.float32),
        "wk": np.asarray(L0_attn_Wk, np.float32),
        "wv": np.asarray(L0_attn_Wv, np.float32),
        "wo": np.asarray(L0_attn_Wo, np.float32),
        "router": np.asarray(L0_ffn_router_weight, np.float32),
        "gsh": np.asarray(L0_ffn_sh_gate, np.float32),
        "ush": np.asarray(L0_ffn_sh_up, np.float32),
        "dsh": np.asarray(L0_ffn_sh_down, np.float32),
        "gr": np.asarray(L0_ffn_r_gate, np.float32),
        "ur": np.asarray(L0_ffn_r_up, np.float32),
        "dr": np.asarray(L0_ffn_r_down, np.float32),
        "mask_c": cn["mask_c"], "mask_e": cn["mask_e"],
        "mask_cb": cn["mask_cb"], "mask_eb": cn["mask_eb"],
        "ones_bf": cn["ones_bf"], "ones_f32r": cn["ones_f32r"],
    }
    in_maps = []
    for c in range(ncore):
        m = dict(shared)
        m["x"] = np.ascontiguousarray(
            x[c * bpc:(c + 1) * bpc].reshape(BS, cfg["D"]))
        in_maps.append(m)
    res = run_bass_kernel_spmd(nc, in_maps, core_ids=list(range(ncore)))
    outs = [np.asarray(r["out"]).reshape(bpc, cfg["S"], cfg["D"])
            for r in res.results]
    return np.concatenate(outs, axis=0)
